# version 04 (final)
# baseline (speedup 1.0000x reference)
import contextlib
import numpy as np

import concourse.bass as bass
import concourse.mybir as mybir
from concourse.bass_utils import run_bass_kernel_spmd

F32 = mybir.dt.float32
BF16 = mybir.dt.bfloat16
ALU = mybir.AluOpType
AF = mybir.ActivationFunctionType
AX = mybir.AxisListType.X

NCORES = 8
T = 4096
D = 1024
NDC = D // 128
TC = 512
NTCH = T // TC
LN_EPS = 1e-5

ACT_CONST_VALS = [0.0, 1.0, -1.0, 0.5, LN_EPS, 1e-6, 1.0 / D, 0.125, 2.0,
                  np.pi, -np.pi, np.pi / 2, np.pi / 4, 1.0 / 127.0, 127.0]

ENGS = ("tensor", "vector", "scalar", "gpsimd", "sync")



class Buf:
    __slots__ = ("w", "r")

    def __init__(self, fence=()):
        self.w = None
        self.r = list(fence)


class Tok:
    __slots__ = ("sem", "seq", "need")

    def __init__(self, sem, seq):
        self.sem = sem
        self.seq = seq
        self.need = False


class Prog:

    def __init__(self, nc, stack):
        self.nc = nc
        self.ops = {e: [] for e in ENGS}
        self.sem = {}
        for e in ENGS:
            self.sem[e] = stack.enter_context(nc.semaphore(f"s_{e}"))
            self.sem[e + "_dma"] = stack.enter_context(nc.semaphore(f"s_{e}d"))
        self.sem["cc"] = stack.enter_context(nc.semaphore("s_cc"))
        self.seq = {k: 0 for k in self.sem}
        self.water = {e: {} for e in ENGS}
        self.drained = {e: 0 for e in ENGS}

    def _dep(self, eng, toks):
        best = {}
        for t in toks:
            if t is None:
                continue
            prev = best.get(t.sem)
            if prev is None or t.seq > prev.seq:
                best[t.sem] = t
        for s, t in best.items():
            if s == eng:
                if t.seq > self.drained[eng]:
                    self.ops[eng].append({"drain": True})
                    self.drained[eng] = self.seq[eng]
                continue
            w = self.water[eng]
            if w.get(s, 0) >= t.seq:
                continue
            w[s] = t.seq
            t.need = True
            self.ops[eng].append({"wait": t})

    def op(self, eng, emit, reads=(), writes=(), dma=False, stream=None):
        deps = []
        for b in reads:
            if b is not None and b.w:
                deps.append(b.w)
        for b in writes:
            if b is None:
                continue
            deps.extend(b.r)
            if b.w:
                deps.append(b.w)
        self._dep(eng, deps)
        if stream is None:
            stream = eng + "_dma" if dma else eng
        self.seq[stream] += 1
        tok = Tok(stream, self.seq[stream])
        if dma or stream == "cc":
            tok.need = True
        self.ops[eng].append({"emit": emit, "tok": tok, "dma": dma})
        for b in reads:
            if b is not None:
                b.r.append(tok)
        for b in writes:
            if b is not None:
                b.w = tok
                b.r = []
        return tok

    def wait_tok(self, eng, tok):
        self._dep(eng, [tok])

    def playback(self):
        nc = self.nc
        val = {k: 0 for k in self.sem}
        seq2val = {k: {} for k in self.sem}
        for e in ENGS:
            for ent in self.ops[e]:
                if "emit" in ent:
                    t = ent["tok"]
                    if t.need:
                        val[t.sem] += 16 if ent["dma"] else 1
                    seq2val[t.sem][t.seq] = val[t.sem]

        def run_stream(e):
            eng = getattr(nc, e)
            for ent in self.ops[e]:
                if "drain" in ent:
                    eng.drain()
                elif "wait" in ent:
                    t = ent["wait"]
                    v = seq2val[t.sem][t.seq]
                    eng.wait_ge(self.sem[t.sem], v)
                else:
                    ins = ent["emit"](eng)
                    t = ent["tok"]
                    if t.need:
                        ins.then_inc(self.sem[t.sem], 16 if ent["dma"] else 1)

        with nc.Block() as block:
            for e in ENGS:
                getattr(block, e)(lambda eng, E=e: run_stream(E))


class Tile:
    __slots__ = ("ap", "buf")

    def __init__(self, ap, buf):
        self.ap = ap
        self.buf = buf

    def __getitem__(self, sl):
        return Tile(self.ap[sl], self.buf)


class Arena:
    def __init__(self, handle, cols):
        self.h = handle
        self.cols = cols
        self.off = 0
        self.fence = []
        self.live = []

    def tile(self, cols, parts=128):
        assert self.off + cols <= self.cols, (
            f"SBUF arena overflow: {self.off}+{cols} > {self.cols}")
        ap = self.h[0:parts, self.off:self.off + cols]
        self.off += cols
        b = Buf(self.fence)
        self.live.append(b)
        return Tile(ap, b)

    def mark(self):
        return self.off, len(self.live), list(self.fence)

    def reset(self, mark=None):
        off0, n0, fence0 = mark if mark else (0, 0, [])
        best = {}
        for t in fence0:
            if t.sem not in best or t.seq > best[t.sem].seq:
                best[t.sem] = t
        for b in self.live[n0:]:
            for t in ([b.w] if b.w else []) + b.r:
                if t.sem not in best or t.seq > best[t.sem].seq:
                    best[t.sem] = t
        self.fence = list(best.values())
        del self.live[n0:]
        self.off = off0



class MK:
    def __init__(self, nc, stack, dev_inputs_meta):
        self.nc = nc
        self.p = Prog(nc, stack)
        self.ext = {}
        for name, (shape, dt, kind) in dev_inputs_meta.items():
            self.ext[name] = nc.dram_tensor(name, list(shape), dt, kind=kind).ap()
        self.dram = {}
        self.dram_bufs = {}
        h = stack.enter_context(nc.sbuf_tensor("arena", [128, 42000], F32))
        self.A = Arena(h, 42000)
        hs = stack.enter_context(nc.sbuf_tensor("smalls", [128, 6000], F32))
        self.S = Arena(hs, 6000)
        hc = stack.enter_context(nc.sbuf_tensor("consts", [128, 1000], F32))
        self.C = Arena(hc, 1000)
        hb = stack.enter_context(nc.sbuf_tensor("bfarena", [128, 8192], BF16))
        self.B = Arena(hb, 8192)
        self.ps = []
        for i in range(8):
            hp = stack.enter_context(nc.psum_tensor(f"ps{i}", [128, 512], F32))
            self.ps.append(Tile(hp[:, :], Buf()))
        self.consts = {}
        self.cc_n = 0
        self.dbg_i = 0
        import os as _os
        av = None if _os.environ.get("NOREG") else self.ext.get("c_actvals")
        if av is not None:
            k = av.shape[1]
            t = self.C.tile(k)
            self.dma("sync", t.ap[0:128, 0:k], av, writes=[t.buf])
            for i, v in enumerate(ACT_CONST_VALS):
                nc.const_aps.register(F32, float(v), t.ap[0:128, i:i + 1])
            self.actconst_buf = t.buf
        else:
            self.actconst_buf = None

    def dbg(self, t, parts=128, cols=None):
        if "dbg%d" % self.dbg_i not in self.ext:
            return
        ap = self.ext["dbg%d" % self.dbg_i]
        self.dbg_i += 1
        cols = cols if cols is not None else t.ap.shape[-1]
        self.p.op("gpsimd", lambda e: e.dma_start(
            out=ap[0:parts, 0:cols], in_=t.ap[0:parts, 0:cols],
            allow_slow_non_contiguous=True),
            reads=[t.buf], writes=[Buf()], dma=True)

    def scratch(self, name, shape, dtype=F32):
        if name not in self.dram:
            self.dram[name] = self.nc.dram_tensor(name, list(shape), dtype).ap()
            self.dram_bufs[name] = Buf()
        return Tile(self.dram[name], self.dram_bufs[name])

    def dma(self, q, out, in_, reads=(), writes=()):
        return self.p.op(q, lambda e: e.dma_start(out=out, in_=in_),
                         reads=reads, writes=writes, dma=True)

    def dma_grouped(self, q, dst_ap, ext_ap, c0, nch, col0, ncols,
                    reads=(), writes=()):
        src = ext_ap.rearrange("(c p) t -> p c t", p=128)
        src = src[:, c0:c0 + nch, col0:col0 + ncols]
        d = dst_ap.rearrange("p (c t) -> p c t", t=ncols)
        return self.p.op(q, lambda e: e.dma_start(out=d, in_=src),
                         reads=reads, writes=writes, dma=True)

    def mm(self, out, lhsT, rhs, start, stop, reads=(), writes=()):
        return self.p.op(
            "tensor", lambda e: e.matmul(out, lhsT, rhs, start=start, stop=stop),
            reads=reads, writes=writes)

    def act(self, out_t, in_t, func, bias=0.0, scale=1.0, accum=None, eng="scalar"):
        reads = [in_t.buf]
        writes = [out_t.buf]
        b = bias
        s = scale
        if isinstance(bias, Tile):
            reads.append(bias.buf)
            b = bias.ap
        if isinstance(scale, Tile):
            reads.append(scale.buf)
            s = scale.ap
        kw = {}
        if accum is not None:
            writes.append(accum.buf)
            kw["accum_out"] = accum.ap
        return self.p.op(eng, lambda e: e.activation(
            out_t.ap, in_t.ap, func, bias=b, scale=s, **kw),
            reads=reads, writes=writes)

    def tt(self, out_t, a, b, op, eng="vector"):
        return self.p.op(eng, lambda e: e.tensor_tensor(out_t.ap, a.ap, b.ap, op),
                         reads=[a.buf, b.buf], writes=[out_t.buf])

    def ts(self, out_t, in_t, s1, s2=None, op0=ALU.mult, op1=None, eng="vector"):
        reads = [in_t.buf]
        a1, a2 = s1, s2
        if isinstance(s1, Tile):
            reads.append(s1.buf)
            a1 = s1.ap
        if isinstance(s2, Tile):
            reads.append(s2.buf)
            a2 = s2.ap
        if op1 is None:
            return self.p.op(eng, lambda e: e.tensor_scalar(
                out_t.ap, in_t.ap, a1, None, op0),
                reads=reads, writes=[out_t.buf])
        return self.p.op(eng, lambda e: e.tensor_scalar(
            out_t.ap, in_t.ap, a1, a2, op0, op1),
            reads=reads, writes=[out_t.buf])

    def stt(self, out_t, in0, s, in1, op0, op1, eng="vector"):
        reads = [in0.buf, in1.buf]
        a = s
        if isinstance(s, Tile):
            reads.append(s.buf)
            a = s.ap
        return self.p.op(eng, lambda e: e.scalar_tensor_tensor(
            out_t.ap, in0.ap, a, in1.ap, op0, op1),
            reads=reads, writes=[out_t.buf])

    def red(self, out_t, in_t, op=ALU.max, absval=False, eng="vector"):
        if op == ALU.max:
            return self.p.op(eng, lambda e: e.reduce_max(
                out_t.ap, in_t.ap, axis=AX, apply_absolute_value=absval),
                reads=[in_t.buf], writes=[out_t.buf])
        return self.p.op(eng, lambda e: e.tensor_reduce(
            out_t.ap, in_t.ap, axis=AX, op=op, apply_absolute_value=absval),
            reads=[in_t.buf], writes=[out_t.buf])

    def copy(self, out_t, in_t, eng="scalar"):
        if eng == "scalar":
            return self.p.op(eng, lambda e: e.copy(out_t.ap, in_t.ap),
                             reads=[in_t.buf], writes=[out_t.buf])
        return self.p.op(eng, lambda e: e.tensor_copy(out_t.ap, in_t.ap),
                         reads=[in_t.buf], writes=[out_t.buf])

    def recip(self, out_t, in_t):
        return self.p.op("vector", lambda e: e.reciprocal(out_t.ap, in_t.ap),
                         reads=[in_t.buf], writes=[out_t.buf])

    def memset(self, t, val, eng="vector"):
        return self.p.op(eng, lambda e: e.memset(t.ap, val), writes=[t.buf])

    def scan(self, out_t, d0, d1, initial, op0, op1):
        reads = [d0.buf, d1.buf]
        init = initial
        if isinstance(initial, Tile):
            reads.append(initial.buf)
            init = initial.ap
        return self.p.op("vector", lambda e: e.tensor_tensor_scan(
            out_t.ap, d0.ap, d1.ap, init, op0, op1),
            reads=reads, writes=[out_t.buf])

    def const(self, name):
        if name in self.consts:
            return self.consts[name]
        ap = self.ext[name]
        pshape, cols = ap.shape[0], ap.shape[1]
        t = self.C.tile(cols)
        tt = Tile(t.ap[0:pshape, 0:cols], t.buf)
        self.dma("sync", tt.ap, ap, writes=[tt.buf])
        self.consts[name] = tt
        return tt

    def aconst(self, name):
        ap = self.ext[name]
        pshape, cols = ap.shape[0], ap.shape[1]
        t = self.A.tile(cols)
        tt = Tile(t.ap[0:pshape, 0:cols], t.buf)
        self.dma("sync", tt.ap, ap, writes=[tt.buf])
        return tt

    def bcast_row(self, row_t, ps_t, cols, to_sbuf=None):
        ones = self.const("c_ones_1x128")
        out = Tile(ps_t.ap[:, 0:cols], ps_t.buf)
        self.mm(out.ap, ones.ap, row_t.ap[0:1, 0:cols], True, True,
                reads=[ones.buf, row_t.buf], writes=[ps_t.buf])
        if to_sbuf is not None:
            dst = Tile(to_sbuf.ap[:, 0:cols], to_sbuf.buf)
            self.copy(dst, out, eng="scalar")
            return dst
        return out

    def colsum_sq(self, x_tiles, tc, ps_t, square):
        n = len(x_tiles)
        onesc = self.const("c_ones_128x1")
        if square:
            tmp = [self.S.tile(tc), self.S.tile(tc)] if tc <= 512 else None
            assert tmp is not None
            for j, xt in enumerate(x_tiles):
                sq = tmp[j % 2]
                self.act(sq, xt, AF.Square)
                self.mm(ps_t.ap[0:1, 0:tc], onesc.ap, sq.ap, j == 0, j == n - 1,
                        reads=[onesc.buf, sq.buf], writes=[ps_t.buf])
        else:
            for j, xt in enumerate(x_tiles):
                self.mm(ps_t.ap[0:1, 0:tc], onesc.ap, xt.ap, j == 0, j == n - 1,
                        reads=[onesc.buf, xt.buf], writes=[ps_t.buf])

    def layernorm_chunk(self, x_tiles, out_tiles, tc, psA, psB, wb=None):
        mkS = self.S.mark()
        self.colsum_sq(x_tiles, tc, psA, square=False)
        self.colsum_sq(x_tiles, tc, psB, square=True)
        def srow(cols):
            t = self.S.tile(cols)
            return Tile(t.ap[0:1, 0:cols], t.buf)

        m = srow(tc)
        e2 = srow(tc)
        self.ts(m, Tile(psA.ap[0:1, 0:tc], psA.buf), 1.0 / D, op0=ALU.mult)
        self.ts(e2, Tile(psB.ap[0:1, 0:tc], psB.buf), 1.0 / D, op0=ALU.mult)
        m2 = srow(tc)
        self.tt(m2, m, m, ALU.mult)
        var = srow(tc)
        self.tt(var, e2, m2, ALU.subtract)
        self.ts(var, var, LN_EPS, op0=ALU.add)
        sd = srow(tc)
        self.act(sd, var, AF.Sqrt)
        r = srow(tc)
        self.recip(r, sd)
        m_rep = self.bcast_row(m, psA, tc)
        r_rep = self.bcast_row(r, psB, tc)
        for j, (xt, ot) in enumerate(zip(x_tiles, out_tiles)):
            self.tt(ot, xt, m_rep, ALU.subtract)
            self.tt(ot, ot, r_rep, ALU.mult)
            if wb is not None:
                wt, bt = wb
                self.ts(ot, ot, Tile(wt.ap[:, j:j + 1], wt.buf),
                        Tile(bt.ap[:, j:j + 1], bt.buf), ALU.mult, ALU.add)
        self.S.reset(mkS)

    def setup_round(self):
        onesc = self.const("c_ones_128x1")
        t = self.S.tile(1)
        self.ts(t, onesc, 2.0 ** -30, op0=ALU.add)
        u = self.S.tile(1)
        self.ts(u, t, 1.0, op0=ALU.subtract)
        m = self.S.tile(1)
        self.ts(m, u, 0.0, op0=ALU.is_gt)
        c2 = self.S.tile(1)
        self.ts(c2, m, 1.5 * 2.0 ** 52, op0=ALU.mult)
        im = self.S.tile(1)
        self.ts(im, m, -1.0, 1.0, ALU.mult, ALU.add)
        c1 = self.S.tile(1)
        self.ts(c1, im, 1.5 * 2.0 ** 23, op0=ALU.mult)
        cc = self.S.tile(1)
        self.tt(cc, c1, c2, ALU.add)
        self.round_col = cc
        self.prec_m = Tile(m.ap[0:1, 0:1], m.buf)

    def round_inplace(self, t, tmp=None):
        self.ts(t, t, self.round_col, op0=ALU.add)
        self.ts(t, t, self.round_col, op0=ALU.subtract)

    def allreduce(self, op, val_t):
        import os
        if os.environ.get("NO_CC"):
            return val_t
        self.cc_n += 1
        n = self.cc_n
        cin = self.scratch(f"cc_in{n}", [1, 1])
        cout = self.scratch(f"cc_out{n}", [NCORES, 1])
        self.dma("gpsimd", cin.ap, val_t.ap[0:1, 0:1],
                 reads=[val_t.buf], writes=[cin.buf])
        self.p.op("gpsimd", lambda e: e.collective_compute(
            "AllGather", ALU.bypass, replica_groups=[list(range(NCORES))],
            ins=[self.dram[f"cc_in{n}"][:, :]],
            outs=[self.dram[f"cc_out{n}"][:, :]]),
            reads=[cin.buf], writes=[cout.buf], stream="cc")
        row = self.S.tile(NCORES)
        self.dma("gpsimd", row.ap[0:1, 0:NCORES],
                 cout.ap.rearrange("a b -> b a"),
                 reads=[cout.buf], writes=[row.buf])
        res = self.S.tile(1)
        self.red(Tile(res.ap[0:1, 0:1], res.buf),
                 Tile(row.ap[0:1, 0:NCORES], row.buf),
                 op=op, absval=False)
        return Tile(res.ap[0:1, 0:1], res.buf)

    def col_from_scalar(self, s_t, psx):
        rep = self.bcast_row(s_t, psx, 1)
        col = self.S.tile(1)
        self.copy(col, rep, eng="scalar")
        return col

    def pmax_to_scalar(self, col_t, tmp_dram_name="pmaxrow"):
        row = self.S.tile(128)
        d = self.scratch(tmp_dram_name, [128, 1])
        self.dma("gpsimd", d.ap, col_t.ap, reads=[col_t.buf], writes=[d.buf])
        self.dma("gpsimd", row.ap[0:1, 0:128], d.ap.rearrange("a b -> b a"),
                 reads=[d.buf], writes=[row.buf])
        out = self.S.tile(1)
        self.red(Tile(out.ap[0:1, 0:1], out.buf), Tile(row.ap[0:1, 0:128], row.buf),
                 op=ALU.max, absval=False)
        return Tile(out.ap[0:1, 0:1], out.buf)

    def emit_ln_to(self, x_in, dst_fn, tc=TC, track_max=None, wb=None):
        mkln = self.A.mark()
        xch = self.A.tile(NDC * tc)
        och = self.A.tile(NDC * tc)
        nt = T // tc
        for tc_i in range(nt):
            t0 = tc_i * tc
            self.dma_grouped("sync", xch.ap, x_in.ap, 0, NDC, t0, tc,
                             reads=[x_in.buf], writes=[xch.buf])
            x_tiles = [Tile(xch.ap[:, dc * tc:(dc + 1) * tc], xch.buf)
                       for dc in range(NDC)]
            out_tiles = [Tile(och.ap[:, dc * tc:(dc + 1) * tc], och.buf)
                         for dc in range(NDC)]
            self.layernorm_chunk(x_tiles, out_tiles, tc, self.ps[6], self.ps[7],
                                 wb=wb)
            if track_max is not None:
                for dc in range(NDC):
                    pm = self.S.tile(1)
                    self.red(pm, out_tiles[dc], op=ALU.max, absval=True)
                    if tc_i == 0 and dc == 0:
                        self.copy(track_max, pm, eng="vector")
                    else:
                        self.tt(track_max, track_max, pm, ALU.max)
            dst_fn(tc_i, out_tiles)
        self.A.reset(mkln)

    def proj_fm(self, y_tiles, wT_ext, n0, n1, out_psum_fn, tc=TC):
        nw = self.A.tile(128 * NDC)
        for j, nc0 in enumerate(range(n0, n1, 128)):
            self.dma_grouped("sync", nw.ap, wT_ext, 0, NDC, nc0, 128,
                             writes=[nw.buf])
            ps = self.ps[j % 2]
            for dc in range(NDC):
                lw = Tile(nw.ap[:, dc * 128:(dc + 1) * 128], nw.buf)
                self.mm(ps.ap[:, 0:tc], lw.ap, y_tiles[dc].ap, dc == 0,
                        dc == NDC - 1, reads=[lw.buf, y_tiles[dc].buf],
                        writes=[ps.buf])
            out_psum_fn(j, Tile(ps.ap[:, 0:tc], ps.buf))

    def attn_causal_head(self, h, q_fm, k_fm, v_tm, o_fm, S, dh=64):
        NQ = S // 128
        NKC = S // 512
        mkA = self.A.mark()
        mkB = self.B.mark()
        mkS = self.S.mark()
        r0, r1 = h * dh, (h + 1) * dh
        q_bf = self.B.tile(S)
        k_bf = self.B.tile(S)
        self.dma("gpsimd", q_bf.ap[0:dh, 0:S], q_fm.ap[r0:r1, 0:S],
                 reads=[q_fm.buf], writes=[q_bf.buf])
        self.dma("gpsimd", k_bf.ap[0:dh, 0:S], k_fm.ap[r0:r1, 0:S],
                 reads=[k_fm.buf], writes=[k_bf.buf])
        mq = self.A.tile(NQ)
        for qb in range(NQ):
            kend = (qb // 4 + 1)
            for kc in range(kend):
                ps = self.ps[kc % 2]
                lq = Tile(q_bf.ap[0:dh, qb * 128:(qb + 1) * 128], q_bf.buf)
                rk = Tile(k_bf.ap[0:dh, kc * 512:(kc + 1) * 512], k_bf.buf)
                self.mm(ps.ap[:, :], lq.ap, rk.ap, True, True,
                        reads=[lq.buf, rk.buf], writes=[ps.buf])
                pm = self.S.tile(1)
                self.red(pm, ps, op=ALU.max)
                dst = Tile(mq.ap[:, qb:qb + 1], mq.buf)
                if kc == 0:
                    self.copy(dst, pm, eng="vector")
                else:
                    self.tt(dst, dst, pm, ALU.max)
        ident = self.const("c_ident")
        psT = self.ps[5]
        self.p.op("tensor", lambda e: e.transpose(
            psT.ap[0:NQ, 0:128], mq.ap[:, 0:NQ], ident.ap),
            reads=[mq.buf, ident.buf], writes=[psT.buf])
        mrow = self.A.tile(S)
        mts = self.A.tile(128)
        self.copy(Tile(mts.ap[0:NQ, 0:128], mts.buf),
                  Tile(psT.ap[0:NQ, 0:128], psT.buf), eng="scalar")
        mb = self.scratch("mrow_b", [32, 128])
        self.dma("gpsimd", mb.ap[0:NQ, :], mts.ap[0:NQ, 0:128],
                 reads=[mts.buf], writes=[mb.buf])
        self.dma("gpsimd", mrow.ap[0:1, 0:S],
                 mb.ap[0:NQ, :].rearrange("a b -> (a b)")[None, :],
                 reads=[mb.buf], writes=[mrow.buf])
        q_aug = self.A.tile(S)
        k_aug = self.A.tile(S)
        self.dma("sync", q_aug.ap[0:dh, 0:S], q_fm.ap[r0:r1, 0:S],
                 reads=[q_fm.buf], writes=[q_aug.buf])
        self.dma("sync", k_aug.ap[0:dh, 0:S], k_fm.ap[r0:r1, 0:S],
                 reads=[k_fm.buf], writes=[k_aug.buf])
        self.ts(Tile(q_aug.ap[dh:dh + 1, 0:S], q_aug.buf),
                Tile(mrow.ap[0:1, 0:S], mrow.buf), -1.0, op0=ALU.mult)
        self.memset(Tile(k_aug.ap[dh:dh + 1, 0:S], k_aug.buf), 1.0)
        NKB = S // 128
        v_aug = self.A.tile(NKB * (dh + 1))
        self.memset(v_aug, 1.0)
        self.p.op("sync", lambda e: e.dma_start(
            out=v_aug.ap.rearrange("p (k j) -> p k j", j=dh + 1)[:, :, 0:dh],
            in_=v_tm.ap.rearrange("(k p) d -> p k d", p=128)[:, 0:NKB, r0:r1]),
            reads=[v_tm.buf], writes=[v_aug.buf], dma=True)
        masks = self._masks
        pbuf = [self.A.tile(512) for _ in range(3)]
        obuf = self.A.tile(512)
        dbuf = self.A.tile(512)
        for qc in range(S // 512):
            pspv = self.ps[4]
            nkb = qc * 4 + 4
            for kb in range(nkb):
                ps2 = self.ps[2 + kb % 2]
                lk = Tile(k_aug.ap[0:dh + 1, kb * 128:(kb + 1) * 128], k_aug.buf)
                rq = Tile(q_aug.ap[0:dh + 1, qc * 512:(qc + 1) * 512], q_aug.buf)
                self.mm(ps2.ap[:, :], lk.ap, rq.ap, True, True,
                        reads=[lk.buf, rq.buf], writes=[ps2.buf])
                pt = pbuf[kb % 3]
                self.act(pt, ps2, AF.Exp)
                if kb >= qc * 4:
                    self.tt(pt, pt, masks[kb - qc * 4], ALU.mult)
                lv = Tile(v_aug.ap[:, kb * (dh + 1):(kb + 1) * (dh + 1)], v_aug.buf)
                self.mm(pspv.ap[0:dh + 1, :], lv.ap, pt.ap, kb == 0,
                        kb == nkb - 1, reads=[lv.buf, pt.buf], writes=[pspv.buf])
            rec = self.S.tile(512)
            self.recip(Tile(rec.ap[0:1, 0:512], rec.buf),
                       Tile(pspv.ap[dh:dh + 1, 0:512], pspv.buf))
            den = self.bcast_row(Tile(rec.ap[0:1, 0:512], rec.buf), self.ps[5],
                                 512, to_sbuf=dbuf)
            ot = Tile(obuf.ap[0:dh, 0:512], obuf.buf)
            self.tt(ot, Tile(pspv.ap[0:dh, 0:512], pspv.buf),
                    Tile(den.ap[0:dh, 0:512], den.buf), ALU.mult)
            self.dma("gpsimd", o_fm.ap[r0:r1, qc * 512:(qc + 1) * 512], ot.ap,
                     reads=[obuf.buf], writes=[o_fm.buf])
        self.A.reset(mkA)
        self.B.reset(mkB)
        self.S.reset(mkS)

    def rmsnorm_chunk(self, x_tiles, out_tiles, tc, psB, w_col, eps=1e-6):
        mkS = self.S.mark()
        self.colsum_sq(x_tiles, tc, psB, square=True)
        e2 = self.S.tile(tc)
        e2r = Tile(e2.ap[0:1, 0:tc], e2.buf)
        self.ts(e2r, Tile(psB.ap[0:1, 0:tc], psB.buf), 1.0 / D, eps,
                ALU.mult, ALU.add)
        sd = self.S.tile(tc)
        sdr = Tile(sd.ap[0:1, 0:tc], sd.buf)
        self.act(sdr, e2r, AF.Sqrt)
        r = self.S.tile(tc)
        rr = Tile(r.ap[0:1, 0:tc], r.buf)
        self.recip(rr, sdr)
        r_rep = self.bcast_row(rr, psB, tc)
        for j, (xt, ot) in enumerate(zip(x_tiles, out_tiles)):
            self.tt(ot, xt, r_rep, ALU.mult)
            self.ts(ot, ot, Tile(w_col.ap[:, j:j + 1], w_col.buf), op0=ALU.mult)
        self.S.reset(mkS)

    def emit_decoder(self, S, x_dram, out_dram, tab_fm, W):
        q_fm = self.scratch("dc_q", [D, T])
        k_fm = self.scratch("dc_k", [D, T])
        v_tm = self.scratch("dc_v", [T, D])
        o_fm = self.scratch("dc_o", [D, T])
        x2_s = self.scratch("dc_x2", [D, T])
        mk0 = self.A.mark()
        nst = S // TC
        rep2x64 = self.aconst("c_rep2x64")
        self._masks = [self.aconst("c_mask%d" % j) for j in range(4)]
        mkA2 = self.A.mark()
        tabl = self.A.tile(S)
        self.dma("sync", tabl.ap[0:64, 0:S], tab_fm.ap[0:64, 0:S],
                 reads=[tab_fm.buf], writes=[tabl.buf])
        tab_fm = Tile(tabl.ap[0:64, 0:S], tabl.buf)
        sgn = self.A.tile(2 * S)
        self.copy(Tile(sgn.ap[0:32, 0:S], sgn.buf),
                  Tile(tab_fm.ap[32:64, 0:S], tab_fm.buf), eng="scalar")
        self.ts(Tile(sgn.ap[0:32, 0:S], sgn.buf),
                Tile(sgn.ap[0:32, 0:S], sgn.buf), -1.0, op0=ALU.mult)
        self.copy(Tile(sgn.ap[32:64, 0:S], sgn.buf),
                  Tile(tab_fm.ap[32:64, 0:S], tab_fm.buf), eng="scalar")
        self.copy(Tile(sgn.ap[0:32, S:2 * S], sgn.buf),
                  Tile(tab_fm.ap[0:32, 0:S], tab_fm.buf), eng="scalar")
        self.copy(Tile(sgn.ap[32:64, S:2 * S], sgn.buf),
                  Tile(tab_fm.ap[0:32, 0:S], tab_fm.buf), eng="scalar")
        self._evtile = self.A.tile(TC)
        ev2 = self.A.tile(TC)
        sw = self.A.tile(TC)
        rot = self.A.tile(TC)

        def qk_rope(tc_i, y_tiles, which, wname, dst, scale):
            t0 = tc_i * TC
            csr = self.ps[4]
            ssr = self.ps[5]
            self.mm(csr.ap[:, :], rep2x64.ap,
                    sgn.ap[0:64, S + t0:S + t0 + TC], True, True,
                    reads=[rep2x64.buf, sgn.buf], writes=[csr.buf])
            self.mm(ssr.ap[:, :], rep2x64.ap, sgn.ap[0:64, t0:t0 + TC],
                    True, True, reads=[rep2x64.buf, sgn.buf], writes=[ssr.buf])

            def ev(j, ps):
                tt = self._evtile
                self.copy(tt, ps, eng="scalar")
                for qd in range(4):
                    src = [32, 0, 96, 64][qd]
                    self.copy(Tile(sw.ap[qd * 32:(qd + 1) * 32, :], sw.buf),
                              Tile(tt.ap[src:src + 32, :], tt.buf), eng="scalar")
                self.tt(rot, tt, csr, ALU.mult)
                self.tt(ev2, sw, ssr, ALU.mult)
                self.tt(rot, rot, ev2, ALU.add)
                if scale != 1.0:
                    self.ts(rot, rot, scale, op0=ALU.mult)
                self.dma("gpsimd", dst.ap[j * 128:(j + 1) * 128, t0:t0 + TC],
                         rot.ap, reads=[rot.buf], writes=[dst.buf])
            self.proj_fm(y_tiles, self.ext[wname], 0, D, ev)

        def consume(tc_i, y_tiles):
            t0 = tc_i * TC
            mkc = self.A.mark()
            qk_rope(tc_i, y_tiles, 0, W["wq"], q_fm, 0.125)
            qk_rope(tc_i, y_tiles, 1, W["wk"], k_fm, 1.0)
            vw = self.A.tile(512 * NDC)
            for n5 in range(2):
                self.dma_grouped("sync", vw.ap, self.ext[W["wv"]], 0, NDC,
                                 n5 * 512, 512, writes=[vw.buf])
                for tq in range(TC // 128):
                    ps = self.ps[2 + tq % 2]
                    for dc in range(NDC):
                        lw = Tile(vw.ap[:, dc * 512:(dc + 1) * 512], vw.buf)
                        ly = Tile(y_tiles[dc].ap[:, tq * 128:(tq + 1) * 128],
                                  y_tiles[dc].buf)
                        self.mm(ps.ap[:, :], ly.ap, lw.ap, dc == 0, dc == NDC - 1,
                                reads=[lw.buf, y_tiles[dc].buf], writes=[ps.buf])
                    tt = self._evtile
                    self.copy(tt, ps, eng="scalar")
                    self.dma("gpsimd",
                             v_tm.ap[t0 + tq * 128:t0 + (tq + 1) * 128,
                                     n5 * 512:(n5 + 1) * 512],
                             tt.ap[0:128, 0:512], reads=[tt.buf], writes=[v_tm.buf])
            self.A.reset(mkc)

        self.emit_rms_to(x_dram, consume, S, W["ln1"])
        self.A.reset(mkA2)
        for h in range(16):
            self.attn_causal_head(h, q_fm, k_fm, v_tm, o_fm, S)

        mk1 = self.A.mark()
        ow = [self.A.tile(128 * NDC) for _ in range(2)]
        och = self.A.tile(TC * NDC)
        xch = self.A.tile(TC * NDC)
        res = [self.A.tile(TC) for _ in range(2)]
        for tc_i in range(nst):
            t0 = tc_i * TC
            self.dma_grouped("sync", och.ap, o_fm.ap, 0, NDC, t0, TC,
                             reads=[o_fm.buf], writes=[och.buf])
            self.dma_grouped("sync", xch.ap, x_dram.ap, 0, NDC, t0, TC,
                             reads=[x_dram.buf], writes=[xch.buf])
            for mc in range(NDC):
                ps = self.ps[mc % 2]
                w = ow[mc % 2]
                self.dma_grouped("sync", w.ap, self.ext[W["wo"]], 0, NDC,
                                 mc * 128, 128, writes=[w.buf])
                for dc in range(NDC):
                    lw = Tile(w.ap[:, dc * 128:(dc + 1) * 128], w.buf)
                    ro = Tile(och.ap[:, dc * TC:(dc + 1) * TC], och.buf)
                    self.mm(ps.ap[:, :], lw.ap, ro.ap, dc == 0, dc == NDC - 1,
                            reads=[w.buf, ro.buf], writes=[ps.buf])
                rr = res[mc % 2]
                xt = Tile(xch.ap[:, mc * TC:(mc + 1) * TC], xch.buf)
                self.tt(rr, xt, ps, ALU.add)
                self.dma("gpsimd", x2_s.ap[mc * 128:(mc + 1) * 128, t0:t0 + TC],
                         rr.ap, reads=[rr.buf], writes=[x2_s.buf])
        self.A.reset(mk1)

        self.A.reset(mk0)
        a_s = self.scratch("dc_a", [4096, T])
        x2t = Tile(self.dram["dc_x2"], self.dram_bufs["dc_x2"])
        h2full = self.A.tile(NDC * S)
        mkh2 = self.A.mark()
        xh2 = self.A.tile(NDC * TC)
        for tc_i in range(S // TC):
            t0 = tc_i * TC
            self.dma_grouped("sync", xh2.ap, x2_s.ap, 0, NDC, t0, TC,
                             reads=[x2_s.buf], writes=[xh2.buf])
            x_tiles = [Tile(xh2.ap[:, dc * TC:(dc + 1) * TC], xh2.buf)
                       for dc in range(NDC)]
            out_tiles = [Tile(h2full.ap[:, dc * S + t0:dc * S + t0 + TC],
                              h2full.buf) for dc in range(NDC)]
            self.rmsnorm_chunk(x_tiles, out_tiles, TC, self.ps[7], W["ln2"])
        self.A.reset(mkh2)
        wgu = [self.A.tile(128 * NDC) for _ in range(4)]
        sg = self.A.tile(TC)
        arow = self.A.tile(S)
        for ncc in range(32):
            wg_ = wgu[(ncc % 2) * 2]
            wu_ = wgu[(ncc % 2) * 2 + 1]
            self.dma_grouped("sync", wg_.ap, self.ext[W["wg"]], 0, NDC,
                             ncc * 128, 128, writes=[wg_.buf])
            self.dma_grouped("sync", wu_.ap, self.ext[W["wu"]], 0, NDC,
                             ncc * 128, 128, writes=[wu_.buf])
            for tc_i in range(S // TC):
                t0 = tc_i * TC
                psg = self.ps[0 + tc_i % 2]
                psu = self.ps[2 + tc_i % 2]
                for dc in range(NDC):
                    h2t = Tile(h2full.ap[:, dc * S + t0:dc * S + t0 + TC],
                               h2full.buf)
                    self.mm(psg.ap[:, :],
                            wg_.ap[:, dc * 128:(dc + 1) * 128], h2t.ap,
                            dc == 0, dc == NDC - 1,
                            reads=[wg_.buf, h2full.buf], writes=[psg.buf])
                    self.mm(psu.ap[:, :],
                            wu_.ap[:, dc * 128:(dc + 1) * 128], h2t.ap,
                            dc == 0, dc == NDC - 1,
                            reads=[wu_.buf, h2full.buf], writes=[psu.buf])
                self.act(sg, psg, AF.Silu)
                at = Tile(arow.ap[:, t0:t0 + TC], arow.buf)
                self.tt(at, sg, psu, ALU.mult)
            self.dma("gpsimd", a_s.ap[ncc * 128:(ncc + 1) * 128, 0:S], arow.ap,
                     reads=[arow.buf], writes=[a_s.buf])
        self.A.reset(mk0)
        ach = self.A.tile(32 * TC)
        x2ch = self.A.tile(NDC * TC)
        wdt = [self.A.tile(128 * 16) for _ in range(2)]
        ot2 = [self.A.tile(TC) for _ in range(2)]
        for tc_i in range(S // TC):
            t0 = tc_i * TC
            self.dma_grouped("sync", ach.ap, a_s.ap, 0, 32, t0, TC,
                             reads=[a_s.buf], writes=[ach.buf])
            self.dma_grouped("sync", x2ch.ap, x2_s.ap, 0, NDC, t0, TC,
                             reads=[x2_s.buf], writes=[x2ch.buf])
            for mc in range(NDC):
                ps = self.ps[4 + mc % 2]
                w = wdt[mc % 2]
                self.dma_grouped("sync", w.ap[:, 0:16 * 128], self.ext[W["wd"]],
                                 0, 16, mc * 128, 128, writes=[w.buf])
                for kc in range(32):
                    if kc == 16:
                        self.dma_grouped("sync", w.ap[:, 0:16 * 128],
                                         self.ext[W["wd"]], 16, 16, mc * 128,
                                         128, writes=[w.buf])
                    lw = Tile(w.ap[:, (kc % 16) * 128:(kc % 16 + 1) * 128],
                              w.buf)
                    ra = Tile(ach.ap[:, kc * TC:(kc + 1) * TC], ach.buf)
                    self.mm(ps.ap[:, :], lw.ap, ra.ap, kc == 0, kc == 31,
                            reads=[w.buf, ra.buf], writes=[ps.buf])
                o = ot2[mc % 2]
                xt = Tile(x2ch.ap[:, mc * TC:(mc + 1) * TC], x2ch.buf)
                self.tt(o, xt, ps, ALU.add)
                self.dma("gpsimd",
                         out_dram.ap[mc * 128:(mc + 1) * 128, t0:t0 + TC],
                         o.ap, reads=[o.buf], writes=[out_dram.buf])
        self.A.reset(mk0)

    def emit_rms_to(self, x_in, dst_fn, S, w_col, eps=1e-6):
        mkln = self.A.mark()
        xch = self.A.tile(NDC * TC)
        och = self.A.tile(NDC * TC)
        for tc_i in range(S // TC):
            t0 = tc_i * TC
            self.dma_grouped("sync", xch.ap, x_in.ap, 0, NDC, t0, TC,
                             reads=[x_in.buf], writes=[xch.buf])
            x_tiles = [Tile(xch.ap[:, dc * TC:(dc + 1) * TC], xch.buf)
                       for dc in range(NDC)]
            out_tiles = [Tile(och.ap[:, dc * TC:(dc + 1) * TC], och.buf)
                         for dc in range(NDC)]
            self.rmsnorm_chunk(x_tiles, out_tiles, TC, self.ps[7], w_col, eps)
            dst_fn(tc_i, out_tiles)
        self.A.reset(mkln)

    def emit_l0attna(self, x_in, x_out):
        DI = 2048
        NJC = DI // 128
        DS = 16
        xin_s = self.scratch("ma_xin", [DI, T])
        z_s = self.scratch("ma_z", [DI, T])
        xc_s = self.scratch("ma_xc", [DI, T])
        y_s = self.scratch("ma_y", [DI, T])
        mk0 = self.A.mark()
        mks = self.S.mark()
        wdt = self.const("L0aa_wdt")
        bdt = self.const("L0aa_bdt")
        dcol = self.const("L0aa_D")
        cvw = self.const("L0aa_convw")
        cvb = self.const("L0aa_convb")
        alog = self.const("L0aa_alog")
        onesr = self.const("c_ones_1x128")
        acol = self.A.tile(256)
        self.act(acol, alog, AF.Exp)
        self.ts(acol, acol, -1.0, op0=ALU.mult)

        self._evtile = self.A.tile(TC)

        def consume(tc_i, y_tiles):
            t0 = tc_i * TC
            mkc = self.A.mark()

            def ev(j, ps):
                tt = self._evtile
                self.copy(tt, ps, eng="scalar")
                dst = xin_s if j < NJC else z_s
                r = (j % NJC) * 128
                self.dma("gpsimd", dst.ap[r:r + 128, t0:t0 + TC], tt.ap,
                         reads=[tt.buf], writes=[dst.buf])
            self.proj_fm(y_tiles, self.ext["L0aa_winT"], 0, 2 * DI, ev)
            self.A.reset(mkc)

        self.emit_ln_to(x_in, consume)
        ssm = self.A.tile(T)
        carry = self.A.tile(NJC * DS)
        mk1 = self.A.mark()

        pad = self.A.tile(T + 4)
        xct = self.A.tile(T)
        wx = self.A.tile(33)
        for j in range(NJC):
            self.memset(Tile(pad.ap[:, 0:3], pad.buf), 0.0)
            self.dma("sync", pad.ap[:, 3:3 + T], xin_s.ap[j * 128:(j + 1) * 128, :],
                     reads=[xin_s.buf], writes=[pad.buf])
            self.ts(xct, Tile(pad.ap[:, 0:T], pad.buf),
                    Tile(cvw.ap[:, j * 4:j * 4 + 1], cvw.buf), op0=ALU.mult)
            for tap in range(1, 4):
                self.stt(xct, Tile(pad.ap[:, tap:tap + T], pad.buf),
                         Tile(cvw.ap[:, j * 4 + tap:j * 4 + tap + 1], cvw.buf),
                         xct, ALU.mult, ALU.add)
            self.act(xct, xct, AF.Silu, bias=Tile(cvb.ap[:, j:j + 1], cvb.buf))
            self.dma("gpsimd", xc_s.ap[j * 128:(j + 1) * 128, :], xct.ap,
                     reads=[xct.buf], writes=[xc_s.buf])
            self.dma("sync", wx.ap[:, 0:33],
                     self.ext["L0aa_wxT"][j * 128:(j + 1) * 128, :],
                     writes=[wx.buf])
            for tc_i in range(NTCH):
                ps = self.ps[tc_i]
                self.mm(ps.ap[0:33, :], wx.ap[:, 0:33],
                        xct.ap[:, tc_i * TC:(tc_i + 1) * TC],
                        j == 0, j == NJC - 1,
                        reads=[wx.buf, xct.buf], writes=[ps.buf])
        for tc_i in range(NTCH):
            self.copy(Tile(ssm.ap[0:33, tc_i * TC:(tc_i + 1) * TC], ssm.buf),
                      Tile(self.ps[tc_i].ap[0:33, :], self.ps[tc_i].buf),
                      eng="scalar")
        self.A.reset(mk1)

        brep = self.A.tile(DS * TC)
        crep = self.A.tile(DS * TC)
        dtrep = self.A.tile(TC)
        stage = self.A.tile(TC)
        xcj = self.A.tile(TC)
        zj = self.A.tile(TC)
        delta = self.A.tile(TC)
        du = self.A.tile(TC)
        t_a = self.A.tile(TC)
        t_b = self.A.tile(TC)
        dab = self.A.tile(TC)
        dbu = self.A.tile(TC)
        hb = self.A.tile(TC)
        yb = self.A.tile(TC)
        yt = self.A.tile(TC)
        for tc_i in range(NTCH):
            t0 = tc_i * TC
            sl = slice(t0, t0 + TC)
            self.copy(Tile(stage.ap[0:1, 0:TC], stage.buf),
                      Tile(ssm.ap[0:1, sl], ssm.buf), eng="vector")
            psb = self.ps[6]
            self.mm(psb.ap[:, :], onesr.ap, stage.ap[0:1, 0:TC], True, True,
                    reads=[onesr.buf, stage.buf], writes=[psb.buf])
            self.copy(dtrep, psb, eng="scalar")
            for s in range(DS):
                for src_row, dst in ((1 + s, brep), (17 + s, crep)):
                    self.dma("gpsimd", stage.ap[0:1, 0:TC],
                             ssm.ap[src_row:src_row + 1, sl],
                             reads=[ssm.buf], writes=[stage.buf])
                    self.mm(psb.ap[:, :], onesr.ap, stage.ap[0:1, 0:TC],
                            True, True, reads=[onesr.buf, stage.buf],
                            writes=[psb.buf])
                    self.copy(Tile(dst.ap[:, s * TC:(s + 1) * TC], dst.buf),
                              psb, eng="scalar")
            for j in range(NJC):
                self.dma("sync", xcj.ap, xc_s.ap[j * 128:(j + 1) * 128, sl],
                         reads=[xc_s.buf], writes=[xcj.buf])
                self.dma("sync", zj.ap, z_s.ap[j * 128:(j + 1) * 128, sl],
                         reads=[z_s.buf], writes=[zj.buf])
                self.ts(delta, dtrep, Tile(wdt.ap[:, j:j + 1], wdt.buf),
                        Tile(bdt.ap[:, j:j + 1], bdt.buf), ALU.mult, ALU.add)
                self.act(t_a, delta, AF.Abs)
                self.act(t_a, t_a, AF.Exp, scale=-1.0)
                self.act(t_a, t_a, AF.Ln, bias=1.0)
                self.act(t_b, delta, AF.Relu)
                self.tt(delta, t_a, t_b, ALU.add)
                self.tt(du, delta, xcj, ALU.mult)
                for s in range(DS):
                    ci = j * DS + s
                    self.act(dab, delta, AF.Exp,
                             scale=Tile(acol.ap[:, ci:ci + 1], acol.buf))
                    self.tt(dbu, du, Tile(brep.ap[:, s * TC:(s + 1) * TC],
                                          brep.buf), ALU.mult)
                    init = (0.0 if tc_i == 0
                            else Tile(carry.ap[:, ci:ci + 1], carry.buf))
                    self.scan(hb, dab, dbu, init, ALU.mult, ALU.add)
                    if tc_i < NTCH - 1:
                        self.copy(Tile(carry.ap[:, ci:ci + 1], carry.buf),
                                  Tile(hb.ap[:, TC - 1:TC], hb.buf),
                                  eng="scalar")
                    if s == 0:
                        self.tt(yb, hb, Tile(crep.ap[:, 0:TC], crep.buf),
                                ALU.mult)
                    else:
                        self.tt(yt, hb, Tile(crep.ap[:, s * TC:(s + 1) * TC],
                                             crep.buf), ALU.mult)
                        self.tt(yb, yb, yt, ALU.add)
                self.stt(yb, xcj, Tile(dcol.ap[:, j:j + 1], dcol.buf), yb,
                         ALU.mult, ALU.add)
                self.act(zj, zj, AF.Silu)
                self.tt(yb, yb, zj, ALU.mult)
                self.dma("gpsimd", y_s.ap[j * 128:(j + 1) * 128, sl], yb.ap,
                         reads=[yb.buf], writes=[y_s.buf])
        self.A.reset(mk1)

        wo = [self.A.tile(128 * NJC) for _ in range(2)]
        ych = self.A.tile(NJC * TC)
        xch = self.A.tile(NDC * TC)
        ot = [self.A.tile(TC) for _ in range(2)]
        for tc_i in range(NTCH):
            t0 = tc_i * TC
            self.dma_grouped("sync", ych.ap, y_s.ap, 0, NJC, t0, TC,
                             reads=[y_s.buf], writes=[ych.buf])
            self.dma_grouped("sync", xch.ap, x_in.ap, 0, NDC, t0, TC,
                             reads=[x_in.buf], writes=[xch.buf])
            for mc in range(NDC):
                ps = self.ps[mc % 2]
                w = wo[mc % 2]
                self.dma_grouped("sync", w.ap, self.ext["L0aa_woutT"], 0, NJC,
                                 mc * 128, 128, writes=[w.buf])
                for kc in range(NJC):
                    lw = Tile(w.ap[:, kc * 128:(kc + 1) * 128], w.buf)
                    ry = Tile(ych.ap[:, kc * TC:(kc + 1) * TC], ych.buf)
                    self.mm(ps.ap[:, :], lw.ap, ry.ap, kc == 0, kc == NJC - 1,
                            reads=[w.buf, ry.buf], writes=[ps.buf])
                o = ot[mc % 2]
                xt = Tile(xch.ap[:, mc * TC:(mc + 1) * TC], xch.buf)
                self.tt(o, xt, ps, ALU.add)
                self.dma("gpsimd", x_out.ap[mc * 128:(mc + 1) * 128, t0:t0 + TC],
                         o.ap, reads=[o.buf], writes=[x_out.buf])
        self.A.reset(mk0)
        self.S.reset(mks)

    def setup_tabs(self):
        if getattr(self, "tabs_done", False):
            return
        self.tabs_done = True
        tfm = self.scratch("tab_fm", [64, T])
        ttm = self.scratch("tab_tm", [T, 64])
        mk = self.A.mark()
        m_col = self.col_from_scalar(self.prec_m, self.ps[6])
        im_col = self.S.tile(1)
        self.ts(im_col, m_col, -1.0, 1.0, ALU.mult, ALU.add)
        a32 = self.A.tile(T)
        ahi = self.A.tile(T)
        alo = self.A.tile(T)
        for nm, dst, parts, nch, w in (("", tfm, 64, 1, T), ("_tm", ttm, 128, 32, 64)):
            for ch in range(nch):
                r0 = ch * 128
                for src, t in (("c_tab32" + nm, a32), ("c_tabhi" + nm, ahi),
                               ("c_tablo" + nm, alo)):
                    ap = self.ext[src]
                    self.dma("sync", t.ap[0:parts, 0:w],
                             ap[r0:r0 + parts, :] if nm else ap[0:parts, 0:w],
                             writes=[t.buf])
                th = Tile(ahi.ap[0:parts, 0:w], ahi.buf)
                tl = Tile(alo.ap[0:parts, 0:w], alo.buf)
                t3 = Tile(a32.ap[0:parts, 0:w], a32.buf)
                self.tt(th, th, tl, ALU.add)
                self.ts(th, th, Tile(m_col.ap[0:parts, 0:1], m_col.buf),
                        op0=ALU.mult)
                self.ts(t3, t3, Tile(im_col.ap[0:parts, 0:1], im_col.buf),
                        op0=ALU.mult)
                self.tt(th, th, t3, ALU.add)
                self.dma("gpsimd",
                         dst.ap[r0:r0 + parts, :] if nm else dst.ap[0:parts, 0:w],
                         th.ap, reads=[ahi.buf], writes=[dst.buf])
        self.A.reset(mk)

    def emit_l0ffn(self, x_in, x_out):
        K = 1024
        hid_s = self.scratch("mf_hid", [D, T])
        proc_s = self.scratch("mf_proc", [D, T])
        proc_tm = self.scratch("mf_ptm", [T, D])
        pt_s = self.scratch("mf_pt", [T, K])
        p_s = self.scratch("mf_p", [K, T])
        sel_s = self.scratch("mf_sel", [D, K])
        y2_s = self.scratch("mf_y2", [D, K])
        dtm_s = self.scratch("mf_dtm", [K, D])
        tab2_s = self.scratch("mf_tab2", [64, K])
        gb_s = self.scratch("mf_gb", [1, T])
        self.setup_tabs()
        mk0 = self.A.mark()
        mks = self.S.mark()
        W1 = {"wq": "Lf_wq1", "wk": "Lf_wk1", "wv": "Lf_wv1", "wo": "Lf_wo1",
              "wg": "Lf_wg1", "wu": "Lf_wu1", "wd": "Lf_wd1",
              "ln1": self.const("Lf_ln1_1"), "ln2": self.const("Lf_ln2_1")}
        W2 = {"wq": "Lf_wq2", "wk": "Lf_wk2", "wv": "Lf_wv2", "wo": "Lf_wo2",
              "wg": "Lf_wg2", "wu": "Lf_wu2", "wd": "Lf_wd2",
              "ln1": self.const("Lf_ln1_2"), "ln2": self.const("Lf_ln2_2")}
        pnw = self.const("Lf_priorn")
        onesr = self.const("c_ones_1x128")
        onesc = self.const("c_ones_128x1")

        def stash(tc_i, y_tiles):
            t0 = tc_i * TC
            for dc in range(NDC):
                self.dma("gpsimd", hid_s.ap[dc * 128:(dc + 1) * 128, t0:t0 + TC],
                         y_tiles[dc].ap, reads=[y_tiles[dc].buf],
                         writes=[hid_s.buf])
        self.emit_ln_to(x_in, stash)

        hid_t = Tile(self.dram["mf_hid"], self.dram_bufs["mf_hid"])
        proc_t = Tile(self.dram["mf_proc"], self.dram_bufs["mf_proc"])
        self.emit_decoder(T, hid_t, proc_t,
                          Tile(self.dram["tab_fm"], self.dram_bufs["tab_fm"]),
                          W1)

        mk1 = self.A.mark()
        ident = self.const("c_ident")
        pch = self.A.tile(NDC * 128)
        tts = self.A.tile(128)
        for tb in range(32):
            for dc in range(NDC):
                self.dma("sync", pch.ap[:, dc * 128:(dc + 1) * 128],
                         proc_s.ap[dc * 128:(dc + 1) * 128,
                                   tb * 128:(tb + 1) * 128],
                         reads=[proc_s.buf], writes=[pch.buf])
            for dc in range(NDC):
                pst = self.ps[dc % 2]
                src = Tile(pch.ap[:, dc * 128:(dc + 1) * 128], pch.buf)
                self.p.op("tensor", lambda e, o=pst.ap[:, 0:128], i=src.ap,
                          d=ident.ap: e.transpose(o, i, d),
                          reads=[src.buf, ident.buf], writes=[pst.buf])
                self.copy(Tile(tts.ap[:, 0:128], tts.buf),
                          Tile(pst.ap[:, 0:128], pst.buf), eng="scalar")
                self.dma("gpsimd",
                         proc_tm.ap[tb * 128:(tb + 1) * 128,
                                    dc * 128:(dc + 1) * 128],
                         tts.ap[:, 0:128], reads=[tts.buf], writes=[proc_tm.buf])
        self.A.reset(mk1)

        mkB = self.A.mark()
        dst_row = self.A.tile(T)
        dch_row = self.A.tile(T)
        g_row = self.A.tile(T)
        mk2 = self.A.mark()
        hch = self.A.tile(NDC * TC)
        pchh = self.A.tile(NDC * TC)
        prh = self.A.tile(NDC * TC)
        ml = self.A.tile(16 * TC)
        sqt = [self.A.tile(TC) for _ in range(2)]
        pw = [self.A.tile(128 * NDC) for _ in range(2)]
        for tc_i in range(NTCH):
            t0 = tc_i * TC
            self.dma_grouped("sync", hch.ap, hid_s.ap, 0, NDC, t0, TC,
                             reads=[hid_s.buf], writes=[hch.buf])
            self.dma_grouped("sync", pchh.ap, proc_s.ap, 0, NDC, t0, TC,
                             reads=[proc_s.buf], writes=[pchh.buf])
            h_tiles = [Tile(hch.ap[:, dc * TC:(dc + 1) * TC], hch.buf)
                       for dc in range(NDC)]
            p_tiles = [Tile(pchh.ap[:, dc * TC:(dc + 1) * TC], pchh.buf)
                       for dc in range(NDC)]
            psA = self.ps[6]
            for dc in range(NDC):
                self.tt(p_tiles[dc], p_tiles[dc], h_tiles[dc], ALU.subtract)
                sq = sqt[dc % 2]
                self.act(sq, p_tiles[dc], AF.Square)
                self.mm(psA.ap[0:1, 0:TC], onesc.ap, sq.ap, dc == 0,
                        dc == NDC - 1, reads=[onesc.buf, sq.buf],
                        writes=[psA.buf])
            self.ts(Tile(dst_row.ap[0:1, t0:t0 + TC], dst_row.buf),
                    Tile(psA.ap[0:1, 0:TC], psA.buf), 1.0 / D, op0=ALU.mult)
            pr_tiles = [Tile(prh.ap[:, dc * TC:(dc + 1) * TC], prh.buf)
                        for dc in range(NDC)]
            self.rmsnorm_chunk(h_tiles, pr_tiles, TC, self.ps[7], pnw)
            for ncc in range(16):
                ps = self.ps[ncc % 2]
                w = pw[ncc % 2]
                self.dma_grouped("sync", w.ap, self.ext["Lf_priorw"], 0, NDC,
                                 ncc * 128, 128, writes=[w.buf])
                for dc in range(NDC):
                    self.mm(ps.ap[:, :],
                            w.ap[:, dc * 128:(dc + 1) * 128], pr_tiles[dc].ap,
                            dc == 0, dc == NDC - 1, reads=[w.buf, prh.buf],
                            writes=[ps.buf])
                self.copy(Tile(ml.ap[:, ncc * TC:(ncc + 1) * TC], ml.buf),
                          ps, eng="scalar")
            psB = self.ps[7]
            for dc in range(NDC):
                mu = Tile(ml.ap[:, dc * TC:(dc + 1) * TC], ml.buf)
                lv = Tile(ml.ap[:, (8 + dc) * TC:(9 + dc) * TC], ml.buf)
                e = sqt[0]
                self.act(e, lv, AF.Exp, scale=-1.0)
                rm = sqt[1]
                self.tt(rm, p_tiles[dc], mu, ALU.subtract)
                self.act(rm, rm, AF.Square)
                self.ts(rm, rm, 1.0, op0=ALU.add)
                self.tt(rm, rm, e, ALU.mult)
                self.mm(psB.ap[0:1, 0:TC], onesc.ap, rm.ap, dc == 0, False,
                        reads=[onesc.buf, rm.buf], writes=[psB.buf])
                self.mm(psB.ap[0:1, 0:TC], onesc.ap, lv.ap, False,
                        dc == NDC - 1, reads=[onesc.buf, ml.buf],
                        writes=[psB.buf])
            self.ts(Tile(dch_row.ap[0:1, t0:t0 + TC], dch_row.buf),
                    Tile(psB.ap[0:1, 0:TC], psB.buf), 0.5 / D, -0.5,
                    ALU.mult, ALU.add)
        self.A.reset(mk2)

        srow = self.S.tile(1)
        self.red(Tile(srow.ap[0:1, 0:1], srow.buf),
                 Tile(dst_row.ap[0:1, 0:T], dst_row.buf), op=ALU.add)
        s1g = self.allreduce(ALU.add, Tile(srow.ap[0:1, 0:1], srow.buf))
        srow2 = self.S.tile(1)
        self.red(Tile(srow2.ap[0:1, 0:1], srow2.buf),
                 Tile(dch_row.ap[0:1, 0:T], dch_row.buf), op=ALU.add)
        s2g = self.allreduce(ALU.add, Tile(srow2.ap[0:1, 0:1], srow2.buf))
        off = self.S.tile(1)
        offr = Tile(off.ap[0:1, 0:1], off.buf)
        self.tt(offr, s1g, s2g, ALU.add)
        self.ts(offr, offr, -1.0 / (NCORES * T), op0=ALU.mult)
        gr = Tile(g_row.ap[0:1, 0:T], g_row.buf)
        self.tt(gr, Tile(dst_row.ap[0:1, 0:T], dst_row.buf),
                Tile(dch_row.ap[0:1, 0:T], dch_row.buf), ALU.add)
        self.act(gr, gr, AF.Sigmoid, bias=offr)
        self.dma("gpsimd", gb_s.ap[0:1, :], gr.ap, reads=[g_row.buf],
                 writes=[gb_s.buf])
        self.A.reset(mkB)

        identb = self.const("c_ident")

        def load_colmat(dram_t, dbuf, dst32):
            stg = self.A.tile(128)
            self.dma("sync", stg.ap[0:32, 0:128],
                     dram_t.rearrange("(c p) -> c p", p=128),
                     reads=[dbuf], writes=[stg.buf])
            pst = self.ps[5]
            self.p.op("tensor", lambda e, o=pst.ap[:, 0:32],
                      i=stg.ap[0:32, 0:128], d=identb.ap[0:32, 0:32]:
                      e.transpose(o, i, d),
                      reads=[stg.buf, identb.buf], writes=[pst.buf])
            self.copy(dst32, Tile(pst.ap[:, 0:32], pst.buf), eng="scalar")

        g2t = self.A.tile(T)
        self.dma("sync", g2t.ap[0:1, 0:T], gb_s.ap[0:1, :],
                 reads=[gb_s.buf], writes=[g2t.buf])
        gr = Tile(g2t.ap[0:1, 0:T], g2t.buf)
        gcm = self.A.tile(32)
        load_colmat(gb_s.ap[0], gb_s.buf, gcm)
        lo = self.S.tile(1)
        hi = self.S.tile(1)
        mid = self.S.tile(1)
        cmask = self.A.tile(32)
        ccol = self.S.tile(1)
        cnt = self.S.tile(1)
        cntsb = self.S.tile(1)
        cm = self.S.tile(1)
        tq = self.S.tile(1)
        self.memset(lo, 0.0)
        self.memset(hi, 1.0)
        self.memset(mid, 0.5)
        psq = self.ps[6]
        for it in range(46):
            self.ts(cmask, gcm, mid, op0=ALU.is_ge)
            self.red(ccol, cmask, op=ALU.add)
            self.mm(psq.ap[0:1, 0:1], onesc.ap, ccol.ap[0:128, 0:1], True, True,
                    reads=[onesc.buf, ccol.buf], writes=[psq.buf])
            self.copy(Tile(cntsb.ap[0:1, 0:1], cntsb.buf),
                      Tile(psq.ap[0:1, 0:1], psq.buf), eng="vector")
            self.mm(psq.ap[:, 0:1], onesr.ap, cntsb.ap[0:1, 0:1], True, True,
                    reads=[onesr.buf, cntsb.buf], writes=[psq.buf])
            self.copy(cnt, Tile(psq.ap[:, 0:1], psq.buf), eng="vector")
            self.ts(cm, cnt, float(K), op0=ALU.is_ge)
            self.tt(tq, mid, lo, ALU.subtract)
            self.tt(tq, tq, cm, ALU.mult)
            self.tt(lo, lo, tq, ALU.add)
            self.tt(tq, mid, hi, ALU.subtract)
            self.ts(cm, cm, -1.0, 1.0, ALU.mult, ALU.add)
            self.tt(tq, tq, cm, ALU.mult)
            self.tt(hi, hi, tq, ALU.add)
            self.tt(mid, lo, hi, ALU.add)
            self.ts(mid, mid, 0.5, op0=ALU.mult)
        mrow = self.A.tile(T)
        rank = self.A.tile(T)
        mkT = self.A.mark()
        mr = Tile(mrow.ap[0:1, 0:T], mrow.buf)
        self.ts(mr, gr, Tile(lo.ap[0:1, 0:1], lo.buf), op0=ALU.is_gt)
        c1 = self.S.tile(1)
        c1r = Tile(c1.ap[0:1, 0:1], c1.buf)
        self.red(c1r, mr, op=ALU.add)
        self.ts(c1r, c1r, -1.0, float(K), ALU.mult, ALU.add)
        tie = self.A.tile(T)
        tier = Tile(tie.ap[0:1, 0:T], tie.buf)
        self.ts(tier, gr, Tile(lo.ap[0:1, 0:1], lo.buf), op0=ALU.is_equal)
        csum = self.A.tile(T)
        csr = Tile(csum.ap[0:1, 0:T], csum.buf)
        self.scan(csr, tier, tier, 0.0, ALU.add, ALU.bypass)
        fill = self.A.tile(T)
        fr = Tile(fill.ap[0:1, 0:T], fill.buf)
        self.ts(fr, csr, Tile(c1.ap[0:1, 0:1], c1.buf), op0=ALU.is_le)
        self.tt(fr, fr, tier, ALU.mult)
        self.tt(mr, mr, fr, ALU.add)
        rr = Tile(rank.ap[0:1, 0:T], rank.buf)
        self.scan(rr, mr, mr, 0.0, ALU.add, ALU.bypass)
        self.A.reset(mkT)
        mb_s = self.scratch("mf_mrb", [1, 3 * T])
        self.dma("gpsimd", mb_s.ap[0:1, 0:T], mr.ap, reads=[mrow.buf],
                 writes=[mb_s.buf])
        self.dma("gpsimd", mb_s.ap[0:1, T:2 * T], rr.ap, reads=[rank.buf],
                 writes=[mb_s.buf])
        mcm = self.A.tile(32)
        rcm = self.A.tile(32)
        load_colmat(mb_s.ap[0][0:T], mb_s.buf, mcm)
        load_colmat(mb_s.ap[0][T:2 * T], mb_s.buf, rcm)
        mkT2 = self.A.mark()
        iorep = self.aconst("c_iota_rep")
        ptt = [self.A.tile(K) for _ in range(2)]
        for tb in range(32):
            pt = ptt[tb % 2]
            self.ts(pt, iorep, Tile(rcm.ap[:, tb:tb + 1], rcm.buf),
                    Tile(mcm.ap[:, tb:tb + 1], mcm.buf), ALU.is_equal, ALU.mult)
            self.dma("gpsimd", pt_s.ap[tb * 128:(tb + 1) * 128, :], pt.ap,
                     reads=[pt.buf], writes=[pt_s.buf])
        self.A.reset(mkT2)
        iocol = self.const("c_iota_col")
        rrep = self.A.tile(T)
        mrep = self.A.tile(T)
        for tc_i in range(NTCH):
            t0 = tc_i * TC
            pr = self.bcast_row(Tile(rank.ap[0:1, t0:t0 + TC], rank.buf),
                                self.ps[6], TC)
            self.copy(Tile(rrep.ap[:, t0:t0 + TC], rrep.buf), pr, eng="scalar")
            pm2 = self.bcast_row(Tile(mrow.ap[0:1, t0:t0 + TC], mrow.buf),
                                 self.ps[7], TC)
            self.copy(Tile(mrep.ap[:, t0:t0 + TC], mrep.buf), pm2, eng="scalar")
        pti = self.A.tile(T)
        for ic in range(8):
            pt = pti
            self.ts(pt, rrep, Tile(iocol.ap[:, ic:ic + 1], iocol.buf),
                    op0=ALU.is_equal)
            self.tt(pt, pt, mrep, ALU.mult)
            self.dma("gpsimd", p_s.ap[ic * 128:(ic + 1) * 128, :], pt.ap,
                     reads=[pt.buf], writes=[p_s.buf])
        self.A.reset(mkT2)
        load_colmat(gb_s.ap[0], gb_s.buf, gcm)
        sctm = self.A.tile(8)
        ptb = [self.A.tile(128) for _ in range(2)]
        for ic in range(8):
            ps = self.ps[2 + ic % 2]
            for tb in range(32):
                pb = ptb[tb % 2]
                self.dma("sync", pb.ap,
                         pt_s.ap[tb * 128:(tb + 1) * 128,
                                 ic * 128:(ic + 1) * 128],
                         reads=[pt_s.buf], writes=[pb.buf])
                self.mm(ps.ap[:, 0:1], pb.ap, gcm.ap[:, tb:tb + 1],
                        tb == 0, tb == 31, reads=[pb.buf, gcm.buf],
                        writes=[ps.buf])
            self.copy(Tile(sctm.ap[:, ic:ic + 1], sctm.buf),
                      Tile(ps.ap[:, 0:1], ps.buf), eng="scalar")
        psc = self.ps[5]
        self.p.op("tensor", lambda e, o=psc.ap[0:8, 0:128],
                  i=sctm.ap[:, 0:8], d=identb.ap: e.transpose(o, i, d),
                  reads=[sctm.buf, identb.buf], writes=[psc.buf])
        sct2 = self.A.tile(128)
        self.copy(Tile(sct2.ap[0:8, 0:128], sct2.buf),
                  Tile(psc.ap[0:8, 0:128], psc.buf), eng="scalar")
        sc_s = self.scratch("mf_scb", [8, 128])
        self.dma("gpsimd", sc_s.ap, sct2.ap[0:8, 0:128], reads=[sct2.buf],
                 writes=[sc_s.buf])
        scr_s = self.scratch("mf_scr", [1, K])
        self.dma("gpsimd", scr_s.ap[0:1, 0:K],
                 sc_s.ap.rearrange("c p -> (c p)")[None, :],
                 reads=[sc_s.buf], writes=[scr_s.buf])

        self._evtile2 = self.A.tile(512)
        ptc = [self.A.tile(512) for _ in range(2)]
        prow = [self.A.tile(NDC * 128 + 64) for _ in range(2)]
        for ic5 in range(2):
            for tb in range(32):
                pb = ptc[tb % 2]
                self.dma("sync", pb.ap,
                         pt_s.ap[tb * 128:(tb + 1) * 128,
                                 ic5 * 512:(ic5 + 1) * 512],
                         reads=[pt_s.buf], writes=[pb.buf])
                pc = prow[tb % 2]
                self.dma("sync", pc.ap[:, 0:NDC * 128],
                         proc_tm.ap[tb * 128:(tb + 1) * 128, :],
                         reads=[proc_tm.buf], writes=[pc.buf])
                self.dma("sync", pc.ap[:, NDC * 128:NDC * 128 + 64],
                         self.dram["tab_tm"][tb * 128:(tb + 1) * 128, :],
                         reads=[self.dram_bufs["tab_tm"]], writes=[pc.buf])
                for dc in range(NDC):
                    ps = self.ps[dc]
                    self.mm(ps.ap[:, :],
                            pc.ap[:, dc * 128:(dc + 1) * 128], pb.ap,
                            tb == 0, tb == 31,
                            reads=[pc.buf, pb.buf], writes=[ps.buf])
            for dc in range(NDC):
                tt = self._evtile2
                self.copy(tt, self.ps[dc], eng="scalar")
                self.dma("gpsimd",
                         sel_s.ap[dc * 128:(dc + 1) * 128,
                                  ic5 * 512:(ic5 + 1) * 512],
                         tt.ap, reads=[tt.buf], writes=[sel_s.buf])
            ps = self.ps[0]
            for tb in range(32):
                pb = ptc[tb % 2]
                self.dma("sync", pb.ap,
                         pt_s.ap[tb * 128:(tb + 1) * 128,
                                 ic5 * 512:(ic5 + 1) * 512],
                         reads=[pt_s.buf], writes=[pb.buf])
                pc = prow[tb % 2]
                self.dma("sync", pc.ap[:, 0:64],
                         self.dram["tab_tm"][tb * 128:(tb + 1) * 128, :],
                         reads=[self.dram_bufs["tab_tm"]], writes=[pc.buf])
                self.mm(ps.ap[0:64, :], pc.ap[:, 0:64], pb.ap,
                        tb == 0, tb == 31, reads=[pc.buf, pb.buf],
                        writes=[ps.buf])
            tt = self._evtile2
            self.copy(Tile(tt.ap[0:64, 0:512], tt.buf),
                      Tile(ps.ap[0:64, :], ps.buf), eng="scalar")
            self.dma("gpsimd", tab2_s.ap[0:64, ic5 * 512:(ic5 + 1) * 512],
                     tt.ap[0:64, 0:512], reads=[tt.buf], writes=[tab2_s.buf])
        self.A.reset(mk0)
        self.S.reset(mks)

        self.emit_decoder(K, Tile(self.dram["mf_sel"], self.dram_bufs["mf_sel"]),
                          Tile(self.dram["mf_y2"], self.dram_bufs["mf_y2"]),
                          tab2_s, W2)
        self.A.reset(mk0)

        scl = self.A.tile(K)
        self.dma("sync", scl.ap[0:1, 0:K], scr_s.ap[0:1, :],
                 reads=[scr_s.buf], writes=[scl.buf])
        ych = self.A.tile(K)
        sch = self.A.tile(K)
        tts2 = self.A.tile(128)
        identc = self.const("c_ident")
        for dc in range(NDC):
            self.dma("sync", ych.ap, y2_s.ap[dc * 128:(dc + 1) * 128, :],
                     reads=[y2_s.buf], writes=[ych.buf])
            self.dma("sync", sch.ap, sel_s.ap[dc * 128:(dc + 1) * 128, :],
                     reads=[sel_s.buf], writes=[sch.buf])
            self.tt(ych, ych, sch, ALU.subtract)
            for i5 in range(2):
                pr = self.bcast_row(Tile(scl.ap[0:1, i5 * 512:(i5 + 1) * 512],
                                         scl.buf), self.ps[6], 512)
                self.tt(Tile(ych.ap[:, i5 * 512:(i5 + 1) * 512], ych.buf),
                        Tile(ych.ap[:, i5 * 512:(i5 + 1) * 512], ych.buf),
                        pr, ALU.mult)
            for ib in range(8):
                pst = self.ps[ib % 2]
                src = Tile(ych.ap[:, ib * 128:(ib + 1) * 128], ych.buf)
                self.p.op("tensor", lambda e, o=pst.ap[:, 0:128], i=src.ap,
                          d=identc.ap: e.transpose(o, i, d),
                          reads=[src.buf, identc.buf], writes=[pst.buf])
                self.copy(Tile(tts2.ap[:, 0:128], tts2.buf),
                          Tile(pst.ap[:, 0:128], pst.buf), eng="scalar")
                self.dma("gpsimd",
                         dtm_s.ap[ib * 128:(ib + 1) * 128,
                                  dc * 128:(dc + 1) * 128],
                         tts2.ap[:, 0:128], reads=[tts2.buf], writes=[dtm_s.buf])
        self.A.reset(mk0)

        dfull = self.A.tile(NDC * 8 * 128)
        pch2 = self.A.tile(8 * TC)
        xch2 = self.A.tile(NDC * TC)
        prch = self.A.tile(NDC * TC)
        ot3 = [self.A.tile(TC) for _ in range(2)]
        for mc in range(NDC):
            self.dma_grouped("sync",
                             dfull.ap[:, mc * 1024:(mc + 1) * 1024],
                             dtm_s.ap, 0, 8, mc * 128, 128,
                             reads=[dtm_s.buf], writes=[dfull.buf])
        for tc_i in range(NTCH):
            t0 = tc_i * TC
            self.dma_grouped("sync", xch2.ap, x_in.ap, 0, NDC, t0, TC,
                             reads=[x_in.buf], writes=[xch2.buf])
            self.dma_grouped("sync", prch.ap, proc_s.ap, 0, NDC, t0, TC,
                             reads=[proc_s.buf], writes=[prch.buf])
            self.dma_grouped("sync", pch2.ap, p_s.ap, 0, 8, t0, TC,
                             reads=[p_s.buf], writes=[pch2.buf])
            for mc in range(NDC):
                ps = self.ps[2 + mc % 2]
                for ic in range(8):
                    lw = Tile(dfull.ap[:, mc * 1024 + ic * 128:
                                       mc * 1024 + (ic + 1) * 128], dfull.buf)
                    pb = Tile(pch2.ap[:, ic * TC:(ic + 1) * TC], pch2.buf)
                    self.mm(ps.ap[:, :], lw.ap, pb.ap, ic == 0, ic == 7,
                            reads=[dfull.buf, pch2.buf], writes=[ps.buf])
                o = ot3[mc % 2]
                xt = Tile(xch2.ap[:, mc * TC:(mc + 1) * TC], xch2.buf)
                pt2 = Tile(prch.ap[:, mc * TC:(mc + 1) * TC], prch.buf)
                self.tt(o, xt, pt2, ALU.add)
                self.tt(o, o, ps, ALU.add)
                self.dma("gpsimd",
                         x_out.ap[mc * 128:(mc + 1) * 128, t0:t0 + TC],
                         o.ap, reads=[o.buf], writes=[x_out.buf])
        self.A.reset(mk0)
        self.S.reset(mks)

    def emit_l1attna(self, x_in, x_out):
        NH = 16
        ln0_s = self.scratch("aa_ln0", [D, T])
        v_fm = self.scratch("aa_v", [D, T])
        hd_s = self.scratch("aa_heads", [D, T])
        x2_s = self.scratch("aa_x2", [D, T])
        mk0 = self.A.mark()
        mks = self.S.mark()
        ln1w = self.const("L1aa_ln1w")
        ln1b = self.const("L1aa_ln1b")
        ln2w = self.const("L1aa_ln2w")
        ln2b = self.const("L1aa_ln2b")
        PI = float(np.pi)
        trig = self.A.tile(6 * T)

        def seg(i):
            return Tile(trig.ap[0:16, i * T:(i + 1) * T], trig.buf)
        self._evtile = self.A.tile(TC)

        def consume(tc_i, l0_tiles):
            t0 = tc_i * TC
            mkc = self.A.mark()
            for dc in range(NDC):
                self.dma("gpsimd", ln0_s.ap[dc * 128:(dc + 1) * 128, t0:t0 + TC],
                         l0_tiles[dc].ap, reads=[l0_tiles[dc].buf],
                         writes=[ln0_s.buf])
            ych = self.A.tile(NDC * TC)
            y_tiles = [Tile(ych.ap[:, dc * TC:(dc + 1) * TC], ych.buf)
                       for dc in range(NDC)]
            self.layernorm_chunk(l0_tiles, y_tiles, TC, self.ps[4], self.ps[5],
                                 wb=(ln1w, ln1b))
            wq = self.A.tile(2 * 16 * NDC)
            for dc in range(NDC):
                self.dma("sync", wq.ap[:, dc * 16:(dc + 1) * 16],
                         self.ext["L1aa_wq"][dc * 128:(dc + 1) * 128, :],
                         writes=[wq.buf])
                self.dma("sync", wq.ap[:, 128 + dc * 16:128 + (dc + 1) * 16],
                         self.ext["L1aa_wk"][dc * 128:(dc + 1) * 128, :],
                         writes=[wq.buf])
            for which, si in ((0, 0), (1, 2)):
                ps = self.ps[which]
                for dc in range(NDC):
                    lw = Tile(wq.ap[:, which * 128 + dc * 16:
                                    which * 128 + dc * 16 + 16], wq.buf)
                    self.mm(ps.ap[0:16, 0:TC], lw.ap, y_tiles[dc].ap,
                            dc == 0, dc == NDC - 1,
                            reads=[lw.buf, ych.buf], writes=[ps.buf])
                self.copy(Tile(trig.ap[0:16, si * T + t0:si * T + t0 + TC],
                               trig.buf),
                          Tile(ps.ap[0:16, 0:TC], ps.buf), eng="scalar")
            self.proj_fm(y_tiles, self.ext["L1aa_wv"], 0, D,
                         lambda j, ps: (self.copy(self._evtile, ps, eng="scalar"),
                                        self.dma("gpsimd",
                                                 v_fm.ap[j * 128:(j + 1) * 128,
                                                         t0:t0 + TC],
                                                 self._evtile.ap,
                                                 reads=[self._evtile.buf],
                                                 writes=[v_fm.buf])))
            self.A.reset(mkc)

        self.emit_ln_to(x_in, consume)
        mk1 = self.A.mark()

        self.act(seg(0), seg(0), AF.Tanh)
        self.ts(seg(0), seg(0), PI / 4, op0=ALU.mult)
        self.act(seg(2), seg(2), AF.Tanh)
        self.ts(seg(2), seg(2), PI / 4, op0=ALU.mult)
        self.act(seg(1), seg(0), AF.Sin)
        self.act(seg(0), seg(0), AF.Sin, bias=PI / 2)
        self.act(seg(3), seg(2), AF.Sin)
        self.act(seg(2), seg(2), AF.Sin, bias=PI / 2)
        self.scan(seg(4), seg(2), seg(2), 0.0, ALU.add, ALU.bypass)
        self.scan(seg(5), seg(3), seg(3), 0.0, ALU.add, ALU.bypass)
        tmp = self.A.tile(T)
        tv = Tile(tmp.ap[0:16, 0:T], tmp.buf)
        tmp2 = self.A.tile(T)
        tv2 = Tile(tmp2.ap[0:16, 0:T], tmp2.buf)
        self.tt(tv, seg(0), seg(4), ALU.mult)
        self.tt(tv2, seg(1), seg(5), ALU.mult)
        self.tt(tv, tv, tv2, ALU.add)
        self.ts(tv, tv, 1e-6, op0=ALU.add)
        self.recip(seg(4), tv)
        self.A.reset(mk1)

        RT = self.aconst("c_rep16")
        vch = self.A.tile(T)
        ckv = self.A.tile(T)
        skv = self.A.tile(T)
        hh = vch
        t1 = self.A.tile(512)
        for j in range(NDC):
            self.dma("sync", vch.ap, v_fm.ap[j * 128:(j + 1) * 128, :],
                     reads=[v_fm.buf], writes=[vch.buf])
            rts = Tile(RT.ap[0:16, j * 128:(j + 1) * 128], RT.buf)
            for tc_i in range(NTCH):
                t0 = tc_i * TC
                sl = slice(t0, t0 + TC)
                for qi, dst in ((2, ckv), (3, skv)):
                    ps = self.ps[qi % 2]
                    self.mm(ps.ap[:, 0:TC], rts.ap,
                            trig.ap[0:16, qi * T + t0:qi * T + t0 + TC],
                            True, True, reads=[RT.buf, trig.buf], writes=[ps.buf])
                    self.tt(Tile(dst.ap[:, sl], dst.buf),
                            Tile(vch.ap[:, sl], vch.buf), ps, ALU.mult)
            self.scan(ckv, ckv, ckv, 0.0, ALU.add, ALU.bypass)
            self.scan(skv, skv, skv, 0.0, ALU.add, ALU.bypass)
            for tc_i in range(NTCH):
                t0 = tc_i * TC
                sl = slice(t0, t0 + TC)
                ps0 = self.ps[0]
                ps1 = self.ps[1]
                ps2 = self.ps[2]
                self.mm(ps0.ap[:, 0:TC], rts.ap,
                        trig.ap[0:16, 0 * T + t0:0 * T + t0 + TC], True, True,
                        reads=[RT.buf, trig.buf], writes=[ps0.buf])
                self.mm(ps1.ap[:, 0:TC], rts.ap,
                        trig.ap[0:16, 1 * T + t0:1 * T + t0 + TC], True, True,
                        reads=[RT.buf, trig.buf], writes=[ps1.buf])
                self.mm(ps2.ap[:, 0:TC], rts.ap,
                        trig.ap[0:16, 4 * T + t0:4 * T + t0 + TC],
                        True, True, reads=[RT.buf, trig.buf], writes=[ps2.buf])
                hv = Tile(hh.ap[:, sl], hh.buf)
                self.tt(hv, Tile(ckv.ap[:, sl], ckv.buf), ps0, ALU.mult)
                self.tt(t1, Tile(skv.ap[:, sl], skv.buf), ps1, ALU.mult)
                self.tt(hv, hv, t1, ALU.add)
                self.tt(hv, hv, ps2, ALU.mult)
            self.dma("gpsimd", hd_s.ap[j * 128:(j + 1) * 128, :], hh.ap,
                     reads=[hh.buf], writes=[hd_s.buf])
        self.A.reset(mk0)

        ow = self.A.tile(NDC * 128)
        hch = self.A.tile(NDC * TC)
        lch = self.A.tile(NDC * TC)
        ot = self.A.tile(TC)
        for tc_i in range(NTCH):
            t0 = tc_i * TC
            self.dma_grouped("sync", hch.ap, hd_s.ap, 0, NDC, t0, TC,
                             reads=[hd_s.buf], writes=[hch.buf])
            self.dma_grouped("sync", lch.ap, ln0_s.ap, 0, NDC, t0, TC,
                             reads=[ln0_s.buf], writes=[lch.buf])
            for mc in range(NDC):
                self.dma_grouped("sync", ow.ap, self.ext["L1aa_wo"], 0, NDC,
                                 mc * 128, 128, writes=[ow.buf])
                ps = self.ps[mc % 2]
                for dc in range(NDC):
                    lw = Tile(ow.ap[:, dc * 128:(dc + 1) * 128], ow.buf)
                    rh = Tile(hch.ap[:, dc * TC:(dc + 1) * TC], hch.buf)
                    self.mm(ps.ap[:, :], lw.ap, rh.ap, dc == 0, dc == NDC - 1,
                            reads=[lw.buf, rh.buf], writes=[ps.buf])
                lt = Tile(lch.ap[:, mc * TC:(mc + 1) * TC], lch.buf)
                self.tt(ot, lt, ps, ALU.add)
                self.dma("gpsimd", x2_s.ap[mc * 128:(mc + 1) * 128, t0:t0 + TC],
                         ot.ap, reads=[ot.buf], writes=[x2_s.buf])
        self.A.reset(mk0)

        b1c = self.const("L1aa_b1")
        b2c = self.const("L1aa_b2")
        x2t = Tile(self.dram["aa_x2"], self.dram_bufs["aa_x2"])
        w1s = [self.A.tile(128 * NDC) for _ in range(2)]
        w2s = [self.A.tile(128 * 16) for _ in range(2)]
        a_t = self.A.tile(32 * TC)
        xch = self.A.tile(NDC * TC)
        x2ch = self.A.tile(NDC * TC)
        ot2 = [self.A.tile(TC) for _ in range(2)]

        def consume2(tc_i, y2_tiles):
            t0 = tc_i * TC
            for ncc in range(32):
                ps = self.ps[ncc % 2]
                w = w1s[ncc % 2]
                self.dma_grouped("sync", w.ap, self.ext["L1aa_w1"], 0, NDC,
                                 ncc * 128, 128, writes=[w.buf])
                for dc in range(NDC):
                    self.mm(ps.ap[:, :], w.ap[:, dc * 128:(dc + 1) * 128],
                            y2_tiles[dc].ap, dc == 0,
                            dc == NDC - 1, reads=[w.buf, y2_tiles[dc].buf],
                            writes=[ps.buf])
                at = Tile(a_t.ap[:, ncc * TC:(ncc + 1) * TC], a_t.buf)
                self.act(at, ps, AF.Gelu_apprx_tanh,
                         bias=Tile(b1c.ap[:, ncc:ncc + 1], b1c.buf))
            self.dma_grouped("sync", xch.ap, x_in.ap, 0, NDC, t0, TC,
                             reads=[x_in.buf], writes=[xch.buf])
            self.dma_grouped("sync", x2ch.ap, x2_s.ap, 0, NDC, t0, TC,
                             reads=[x2_s.buf], writes=[x2ch.buf])
            for mc in range(NDC):
                ps = self.ps[2 + mc % 2]
                w = w2s[mc % 2]
                for kc in range(32):
                    if kc % 16 == 0:
                        self.dma_grouped("sync", w.ap, self.ext["L1aa_w2"],
                                         kc, 16, mc * 128, 128, writes=[w.buf])
                    lw = Tile(w.ap[:, (kc % 16) * 128:(kc % 16 + 1) * 128],
                              w.buf)
                    ra = Tile(a_t.ap[:, kc * TC:(kc + 1) * TC], a_t.buf)
                    self.mm(ps.ap[:, :], lw.ap, ra.ap, kc == 0, kc == 31,
                            reads=[w.buf, ra.buf], writes=[ps.buf])
                o = ot2[mc % 2]
                self.act(o, ps, AF.Identity,
                         bias=Tile(b2c.ap[:, mc:mc + 1], b2c.buf))
                xt = Tile(xch.ap[:, mc * TC:(mc + 1) * TC], xch.buf)
                x2c = Tile(x2ch.ap[:, mc * TC:(mc + 1) * TC], x2ch.buf)
                self.tt(o, o, xt, ALU.add)
                self.tt(o, o, x2c, ALU.add)
                self.dma("gpsimd", x_out.ap[mc * 128:(mc + 1) * 128, t0:t0 + TC],
                         o.ap, reads=[o.buf], writes=[x_out.buf])

        self.emit_ln_to(x2t, consume2, wb=(ln2w, ln2b))
        self.A.reset(mk0)
        self.S.reset(mks)

    def emit_l0attnb(self, x_in, x_out):
        f_fm = self.scratch("ab_f", [D, T])
        g_fm = self.scratch("ab_g", [D, T])
        h_tm = self.scratch("ab_h", [T, D])
        ln_s = self.scratch("ab_ln", [D, T])
        mk0 = self.A.mark()
        mks = self.S.mark()
        mkb = self.B.mark()

        def consume(tc_i, y_tiles):
            t0 = tc_i * TC
            mkc = self.A.mark()

            def ev(dst):
                def f(j, ps):
                    tt = self._evtile
                    self.copy(tt, ps, eng="scalar")
                    self.dma("gpsimd", dst.ap[j * 128:(j + 1) * 128, t0:t0 + TC],
                             tt.ap, reads=[tt.buf], writes=[dst.buf])
                return f
            self.proj_fm(y_tiles, self.ext["L0ab_wq"], 0, D, ev(f_fm))
            self.proj_fm(y_tiles, self.ext["L0ab_wk"], 0, D, ev(g_fm))
            vw = self.A.tile(512 * NDC)
            for n5 in range(2):
                for dc in range(NDC):
                    self.dma("sync", vw.ap[:, dc * 512:(dc + 1) * 512],
                             self.ext["L0ab_wv"][dc * 128:(dc + 1) * 128,
                                                 n5 * 512:(n5 + 1) * 512],
                             writes=[vw.buf])
                for tq in range(TC // 128):
                    ps = self.ps[2 + tq % 2]
                    for dc in range(NDC):
                        lw = Tile(vw.ap[:, dc * 512:(dc + 1) * 512], vw.buf)
                        ly = Tile(y_tiles[dc].ap[:, tq * 128:(tq + 1) * 128],
                                  y_tiles[dc].buf)
                        self.mm(ps.ap[:, :], ly.ap, lw.ap, dc == 0, dc == NDC - 1,
                                reads=[lw.buf, y_tiles[dc].buf], writes=[ps.buf])
                    tt = self._evtile
                    self.copy(tt, ps, eng="scalar")
                    self.dma("gpsimd",
                             h_tm.ap[t0 + tq * 128:t0 + (tq + 1) * 128,
                                     n5 * 512:(n5 + 1) * 512],
                             tt.ap[0:128, 0:512], reads=[tt.buf], writes=[h_tm.buf])
            for dc in range(NDC):
                self.dma("gpsimd", ln_s.ap[dc * 128:(dc + 1) * 128, t0:t0 + TC],
                         y_tiles[dc].ap, reads=[y_tiles[dc].buf],
                         writes=[ln_s.buf])
            self.A.reset(mkc)

        self._evtile = self.A.tile(TC)
        self.emit_ln_to(x_in, consume)
        self.A.reset(mk0)

        NQ = T // 128
        mq = self.A.tile(NQ)
        fb = self.B.tile(NDC * 512)
        gb = self.B.tile(NDC * 512)
        for sc in range(8):
            self.p.op("gpsimd", lambda e, d=fb.ap, s0=sc: e.dma_start(
                out=d.rearrange("p (c t) -> p c t", t=512),
                in_=f_fm.ap.rearrange("(c p) t -> p c t", p=128)[
                    :, :, s0 * 512:(s0 + 1) * 512]),
                reads=[f_fm.buf], writes=[fb.buf], dma=True)
            for tbg in range(8):
                self.p.op("gpsimd", lambda e, d=gb.ap, t0=tbg: e.dma_start(
                    out=d.rearrange("p (c t) -> p c t", t=512),
                    in_=g_fm.ap.rearrange("(c p) t -> p c t", p=128)[
                        :, :, t0 * 512:(t0 + 1) * 512]),
                    reads=[g_fm.buf], writes=[gb.buf], dma=True)
                for tj in range(4):
                    tb = tbg * 4 + tj
                    ps = self.ps[tj % 2]
                    for dc in range(NDC):
                        lg = Tile(gb.ap[:, dc * 512 + tj * 128:
                                        dc * 512 + (tj + 1) * 128], gb.buf)
                        rf = Tile(fb.ap[:, dc * 512:(dc + 1) * 512], fb.buf)
                        self.mm(ps.ap[:, :], lg.ap, rf.ap, dc == 0, dc == NDC - 1,
                                reads=[lg.buf, rf.buf], writes=[ps.buf])
                    pm = self.S.tile(1)
                    self.red(pm, ps, op=ALU.max)
                    dst = Tile(mq.ap[:, tb:tb + 1], mq.buf)
                    if sc == 0:
                        self.copy(dst, pm, eng="vector")
                    else:
                        self.tt(dst, dst, pm, ALU.max)
        ident = self.const("c_ident")
        psT = self.ps[5]
        self.p.op("tensor", lambda e: e.transpose(
            psT.ap[0:NQ, 0:128], mq.ap[:, 0:NQ], ident.ap),
            reads=[mq.buf, ident.buf], writes=[psT.buf])
        mts = self.A.tile(128)
        self.copy(Tile(mts.ap[0:NQ, 0:128], mts.buf),
                  Tile(psT.ap[0:NQ, 0:128], psT.buf), eng="scalar")
        mb = self.scratch("mrow_b", [32, 128])
        self.dma("gpsimd", mb.ap[0:NQ, :], mts.ap[0:NQ, 0:128],
                 reads=[mts.buf], writes=[mb.buf])
        mrow = self.A.tile(T)
        self.dma("gpsimd", mrow.ap[0:1, 0:T],
                 mb.ap[0:NQ, :].rearrange("a b -> (a b)")[None, :],
                 reads=[mb.buf], writes=[mrow.buf])
        self.ts(Tile(mrow.ap[0:1, 0:T], mrow.buf),
                Tile(mrow.ap[0:1, 0:T], mrow.buf), -1.0, op0=ALU.mult)
        gam = self.const("L0ab_gamma")

        onesr = self.const("c_ones_1x128")
        onesc = self.const("c_ones_128x1")
        beta = self.A.tile(32 * TC)
        gch = self.A.tile(NDC * TC)
        fch = self.A.tile(NDC * 128)
        hch = self.A.tile(32 * 128)
        xch = self.A.tile(NDC * TC)
        lnch = self.A.tile(NDC * TC)
        ot = self.A.tile(TC)
        ot2 = self.A.tile(TC)
        rd = self.S.tile(TC)
        for tc_i in range(NTCH):
            t0 = tc_i * TC
            self.dma_grouped("sync", gch.ap, g_fm.ap, 0, NDC, t0, TC,
                             reads=[g_fm.buf], writes=[gch.buf])
            for sc in range(32):
                self.dma_grouped("sync", fch.ap, f_fm.ap, 0, NDC,
                                 sc * 128, 128,
                                 reads=[f_fm.buf], writes=[fch.buf])
                ps2 = self.ps[2 + sc % 2]
                for dc in range(NDC):
                    lf = Tile(fch.ap[:, dc * 128:(dc + 1) * 128], fch.buf)
                    rg = Tile(gch.ap[:, dc * TC:(dc + 1) * TC], gch.buf)
                    self.mm(ps2.ap[:, 0:TC], lf.ap, rg.ap, dc == 0, False,
                            reads=[lf.buf, rg.buf], writes=[ps2.buf])
                self.mm(ps2.ap[:, 0:TC], onesr.ap[0:1, 0:128],
                        mrow.ap[0:1, t0:t0 + TC], False, True,
                        reads=[onesr.buf, mrow.buf], writes=[ps2.buf])
                bt = Tile(beta.ap[:, sc * TC:(sc + 1) * TC], beta.buf)
                self.act(bt, ps2, AF.Exp)
            den_ps = self.ps[5]
            for sc in range(32):
                self.mm(den_ps.ap[0:1, 0:TC], onesc.ap,
                        beta.ap[:, sc * TC:(sc + 1) * TC], sc == 0, sc == 31,
                        reads=[onesc.buf, beta.buf], writes=[den_ps.buf])
            self.recip(Tile(rd.ap[0:1, 0:TC], rd.buf),
                       Tile(den_ps.ap[0:1, 0:TC], den_ps.buf))
            self.ts(Tile(rd.ap[0:1, 0:TC], rd.buf),
                    Tile(rd.ap[0:1, 0:TC], rd.buf),
                    Tile(gam.ap[0:1, 0:1], gam.buf), op0=ALU.mult)
            rd_rep = self.bcast_row(Tile(rd.ap[0:1, 0:TC], rd.buf), self.ps[6],
                                    TC, to_sbuf=ot2)
            self.dma_grouped("sync", xch.ap, x_in.ap, 0, NDC, t0, TC,
                             reads=[x_in.buf], writes=[xch.buf])
            self.dma_grouped("sync", lnch.ap, ln_s.ap, 0, NDC, t0, TC,
                             reads=[ln_s.buf], writes=[lnch.buf])
            for cc in range(NDC):
                pspv = self.ps[cc % 2]
                self.dma_grouped("sync", hch.ap, h_tm.ap, 0, 32,
                                 cc * 128, 128,
                                 reads=[h_tm.buf], writes=[hch.buf])
                for sc in range(32):
                    bt = Tile(beta.ap[:, sc * TC:(sc + 1) * TC], beta.buf)
                    self.mm(pspv.ap[:, 0:TC],
                            hch.ap[:, sc * 128:(sc + 1) * 128],
                            bt.ap, sc == 0, sc == 31,
                            reads=[hch.buf, beta.buf], writes=[pspv.buf])
                self.tt(ot, pspv, rd_rep, ALU.mult)
                xt = Tile(xch.ap[:, cc * TC:(cc + 1) * TC], xch.buf)
                lt = Tile(lnch.ap[:, cc * TC:(cc + 1) * TC], lnch.buf)
                self.tt(ot, ot, xt, ALU.add)
                self.tt(ot, ot, lt, ALU.add)
                self.dma("gpsimd", x_out.ap[cc * 128:(cc + 1) * 128, t0:t0 + TC],
                         ot.ap, reads=[ot.buf], writes=[x_out.buf])
        self.A.reset(mk0)
        self.S.reset(mks)
        self.B.reset(mkb)

    def emit_l1attnb(self, x_in, x_out):
        p = self.p
        q_fm = self.scratch("at_q", [D, T])
        k_fm = self.scratch("at_k", [D, T])
        v_tm = self.scratch("at_v", [T, D])
        o_fm = self.scratch("at_o", [D, T])
        mk0 = self.A.mark()
        mks = self.S.mark()

        def consume(tc_i, y_tiles):
            t0 = tc_i * TC
            mkc = self.A.mark()
            def evq(j, ps):
                tt = self._evtile
                self.act(tt, ps, AF.Identity, scale=0.125)
                self.dma("gpsimd", q_fm.ap[j * 128:(j + 1) * 128, t0:t0 + TC],
                         tt.ap, reads=[tt.buf], writes=[q_fm.buf])

            def evk(j, ps):
                tt = self._evtile
                self.copy(tt, ps, eng="scalar")
                self.dma("gpsimd", k_fm.ap[j * 128:(j + 1) * 128, t0:t0 + TC],
                         tt.ap, reads=[tt.buf], writes=[k_fm.buf])
            self.proj_fm(y_tiles, self.ext["L1ab_wq"], 0, D, evq)
            self.proj_fm(y_tiles, self.ext["L1ab_wk"], 0, D, evk)
            vw = self.A.tile(512 * NDC)
            for n5 in range(2):
                for dc in range(NDC):
                    self.dma("sync", vw.ap[:, dc * 512:(dc + 1) * 512],
                             self.ext["L1ab_wv"][dc * 128:(dc + 1) * 128,
                                                 n5 * 512:(n5 + 1) * 512],
                             writes=[vw.buf])
                for tq in range(TC // 128):
                    ps = self.ps[2 + tq % 2]
                    for dc in range(NDC):
                        lw = Tile(vw.ap[:, dc * 512:(dc + 1) * 512], vw.buf)
                        ly = Tile(y_tiles[dc].ap[:, tq * 128:(tq + 1) * 128],
                                  y_tiles[dc].buf)
                        self.mm(ps.ap[:, :], ly.ap, lw.ap, dc == 0, dc == NDC - 1,
                                reads=[lw.buf, y_tiles[dc].buf], writes=[ps.buf])
                    tt = self._evtile
                    self.copy(tt, ps, eng="scalar")
                    self.dma("gpsimd",
                             v_tm.ap[t0 + tq * 128:t0 + (tq + 1) * 128,
                                     n5 * 512:(n5 + 1) * 512],
                             tt.ap[0:128, 0:512], reads=[tt.buf], writes=[v_tm.buf])
            self.A.reset(mkc)

        self._evtile = self.A.tile(TC)
        self.emit_ln_to(x_in, consume)
        self.A.reset(mk0)

        self._masks = [self.aconst("c_mask%d" % j) for j in range(4)]
        for h in range(16):
            self.attn_causal_head(h, q_fm, k_fm, v_tm, o_fm, T)

        ow = self.A.tile(128 * NDC)
        och = self.A.tile(TC * NDC)
        xch = self.A.tile(TC * NDC)
        res = [self.A.tile(TC) for _ in range(2)]
        for tc_i in range(NTCH):
            t0 = tc_i * TC
            for dc in range(NDC):
                self.dma("sync", och.ap[:, dc * TC:(dc + 1) * TC],
                         o_fm.ap[dc * 128:(dc + 1) * 128, t0:t0 + TC],
                         reads=[o_fm.buf], writes=[och.buf])
                self.dma("sync", xch.ap[:, dc * TC:(dc + 1) * TC],
                         x_in.ap[dc * 128:(dc + 1) * 128, t0:t0 + TC],
                         reads=[x_in.buf], writes=[xch.buf])
            for mc in range(NDC):
                for dc in range(NDC):
                    self.dma("sync", ow.ap[:, dc * 128:(dc + 1) * 128],
                             self.ext["L1ab_wo"][dc * 128:(dc + 1) * 128,
                                                 mc * 128:(mc + 1) * 128],
                             writes=[ow.buf])
                ps = self.ps[mc % 2]
                for dc in range(NDC):
                    lw = Tile(ow.ap[:, dc * 128:(dc + 1) * 128], ow.buf)
                    ro = Tile(och.ap[:, dc * TC:(dc + 1) * TC], och.buf)
                    self.mm(ps.ap[:, :], lw.ap, ro.ap, dc == 0, dc == NDC - 1,
                            reads=[lw.buf, ro.buf], writes=[ps.buf])
                rr = res[mc % 2]
                xt = Tile(xch.ap[:, mc * TC:(mc + 1) * TC], xch.buf)
                self.tt(rr, xt, ps, ALU.add)
                self.dma("gpsimd", x_out.ap[mc * 128:(mc + 1) * 128, t0:t0 + TC],
                         rr.ap, reads=[rr.buf], writes=[x_out.buf])
        self.A.reset(mk0)
        self.S.reset(mks)

    def _copy_to_out(self, lnx, x_out):
        for dc in range(NDC):
            for tc_i in range(NTCH):
                t0 = tc_i * TC
                self.dma("gpsimd", x_out.ap[dc * 128:(dc + 1) * 128, t0:t0 + TC],
                         lnx.ap[:, dc * T + t0:dc * T + t0 + TC],
                         reads=[lnx.buf], writes=[x_out.buf])

    def quantize_weight(self, nat_name, tr_name, R, C, out_dram_name):
        nat = self.ext[nat_name]
        tr = self.ext[tr_name]
        nr = R // 128
        mk = self.A.mark()
        mks = self.S.mark()
        scale_row = self.A.tile(R)
        wtile = self.A.tile(C)
        for rc in range(nr):
            self.dma("sync", wtile.ap, nat[rc * 128:(rc + 1) * 128, :],
                     writes=[wtile.buf])
            cmax = self.S.tile(1)
            self.red(cmax, wtile, op=ALU.max, absval=True)
            d = self.scratch(f"qwb_{out_dram_name}", [128, 1])
            db = Tile(d.ap, self.dram_bufs[f"qwb_{out_dram_name}"])
            self.dma("gpsimd", db.ap, cmax.ap[0:128, 0:1],
                     reads=[cmax.buf], writes=[db.buf])
            self.dma("gpsimd",
                     scale_row.ap[0:1, rc * 128:(rc + 1) * 128],
                     db.ap.rearrange("a b -> b a"),
                     reads=[db.buf], writes=[scale_row.buf])
        inv_row = self.A.tile(R)
        s127_row = self.A.tile(R)
        self.recip(Tile(inv_row.ap[0:1, 0:R], inv_row.buf),
                   Tile(scale_row.ap[0:1, 0:R], scale_row.buf))
        self.ts(Tile(inv_row.ap[0:1, 0:R], inv_row.buf),
                Tile(inv_row.ap[0:1, 0:R], inv_row.buf), 127.0, op0=ALU.mult)
        self.ts(Tile(s127_row.ap[0:1, 0:R], s127_row.buf),
                Tile(scale_row.ap[0:1, 0:R], scale_row.buf), 1.0 / 127.0,
                op0=ALU.mult)
        out_d = self.scratch(out_dram_name, [C, R])
        ncc = C // 128
        w2 = [self.A.tile(512), self.A.tile(512)]
        tmp = self.A.tile(512)
        for j0 in range(0, R, 512):
            jw = min(512, R - j0)
            inv_rep = self.bcast_row(
                Tile(inv_row.ap[0:1, j0:j0 + jw], inv_row.buf), self.ps[6], jw,
                to_sbuf=self.A.tile(jw))
            s_rep = self.bcast_row(
                Tile(s127_row.ap[0:1, j0:j0 + jw], s127_row.buf), self.ps[7], jw,
                to_sbuf=self.A.tile(jw))
            for cc in range(ncc):
                wt = Tile(w2[cc % 2].ap[:, 0:jw], w2[cc % 2].buf)
                self.dma("sync", wt.ap, tr[cc * 128:(cc + 1) * 128, j0:j0 + jw],
                         writes=[wt.buf])
                self.tt(wt, wt, inv_rep, ALU.mult)
                tmpt = Tile(tmp.ap[:, 0:jw], tmp.buf)
                self.round_inplace(wt, tmpt)
                self.ts(wt, wt, -128.0, 127.0, ALU.max, ALU.min)
                self.tt(wt, wt, s_rep, ALU.mult)
                self.dma("gpsimd", out_d.ap[cc * 128:(cc + 1) * 128, j0:j0 + jw],
                         wt.ap, reads=[wt.buf], writes=[out_d.buf])
        self.A.reset(mk)
        self.S.reset(mks)
        return out_d

    def quant_scales(self, gmax_t, clip_name):
        clip = self.const(clip_name)
        sig = self.S.tile(1)
        self.act(Tile(sig.ap[0:1, 0:1], sig.buf),
                 Tile(clip.ap[0:1, 0:1], clip.buf), AF.Sigmoid)
        sc = self.S.tile(1)
        self.tt(Tile(sc.ap[0:1, 0:1], sc.buf), Tile(sig.ap[0:1, 0:1], sig.buf),
                gmax_t, ALU.mult)
        inv = self.S.tile(1)
        self.recip(Tile(inv.ap[0:1, 0:1], inv.buf), Tile(sc.ap[0:1, 0:1], sc.buf))
        self.ts(Tile(inv.ap[0:1, 0:1], inv.buf), Tile(inv.ap[0:1, 0:1], inv.buf),
                127.0, op0=ALU.mult)
        self.ts(Tile(sc.ap[0:1, 0:1], sc.buf), Tile(sc.ap[0:1, 0:1], sc.buf),
                1.0 / 127.0, op0=ALU.mult)
        inv_col = self.col_from_scalar(Tile(inv.ap[0:1, 0:1], inv.buf), self.ps[6])
        sc_col = self.col_from_scalar(Tile(sc.ap[0:1, 0:1], sc.buf), self.ps[7])
        return inv_col, sc_col

    def fake_quant_tile(self, t, tmp, inv_col, sc_col):
        self.ts(t, t, inv_col, op0=ALU.mult)
        self.round_inplace(t, tmp)
        self.ts(t, t, -128.0, 127.0, ALU.max, ALU.min)
        self.ts(t, t, sc_col, op0=ALU.mult)

    def emit_l1ffn(self, x_in, x_out, first, stage=9):
        p = self.p
        I = 2816
        NIC = I // 128
        import os as _os
        if stage >= 1 and first and not _os.environ.get("SKIP_WQ"):
            self.wgq = self.quantize_weight("L1w_gate", "L1w_gateT", I, D, "wgq")
            self.wuq = self.quantize_weight("L1w_up", "L1w_upT", I, D, "wuq")
            self.wdq = self.quantize_weight("L1w_down", "L1w_downT", D, I, "wdq")
        mk0 = self.A.mark()
        mks = self.S.mark()

        lnx = self.A.tile(NDC * T)
        mk1 = self.A.mark()
        xch = self.A.tile(NDC * TC)
        mxcol = self.S.tile(1)
        for tc_i in range(int(_os.environ.get("LN_CHUNKS", NTCH))):
            t0 = tc_i * TC
            self.dma_grouped("sync", xch.ap, x_in.ap, 0, NDC, t0, TC,
                             reads=[x_in.buf], writes=[xch.buf])
            x_tiles = [Tile(xch.ap[:, dc * TC:(dc + 1) * TC], xch.buf)
                       for dc in range(NDC)]
            out_tiles = [Tile(lnx.ap[:, dc * T + t0:dc * T + t0 + TC], lnx.buf)
                         for dc in range(NDC)]
            self.layernorm_chunk(x_tiles, out_tiles, TC, self.ps[6], self.ps[7])
            for dc in range(NDC):
                pm = self.S.tile(1)
                self.red(pm, out_tiles[dc], op=ALU.max, absval=True)
                if tc_i == 0 and dc == 0:
                    self.copy(mxcol, pm, eng="vector")
                else:
                    self.tt(mxcol, mxcol, pm, ALU.max)
        self.dbg(self.round_col)
        self.dbg(Tile(lnx.ap[:, 0:512], lnx.buf))
        if stage < 2:
            self._copy_to_out(lnx, x_out)
            self.A.reset(mk0)
            self.S.reset(mks)
            return
        import os
        if os.environ.get("SKIP_PMAX"):
            gmax_loc = Tile(mxcol.ap[0:1, 0:1], mxcol.buf)
        else:
            gmax_loc = self.pmax_to_scalar(mxcol, "pm_l1f_x")
        gmax = self.allreduce(ALU.max, gmax_loc)
        self.dbg(mxcol)
        self.dbg(gmax_loc, parts=1)
        self.dbg(gmax, parts=1)
        if stage < 2.2:
            self._copy_to_out(lnx, x_out)
            self.A.reset(mk0)
            self.S.reset(mks)
            return
        inv_col, sc_col = self.quant_scales(gmax, "L1clip_x")
        self.dbg(inv_col)
        self.dbg(sc_col)
        self.A.reset(mk1)
        qtmp = self.A.tile(512)
        for k in range(NDC * NTCH):
            tt = Tile(lnx.ap[:, k * 512:(k + 1) * 512], lnx.buf)
            self.fake_quant_tile(tt, qtmp, inv_col, sc_col)
        self.dbg(Tile(lnx.ap[:, 0:512], lnx.buf))
        if stage < 3:
            self._copy_to_out(lnx, x_out)
            self.A.reset(mk0)
            self.S.reset(mks)
            return

        self.A.reset(mk1)
        h_d = self.scratch("l1f_h", [I, T])
        wg_t = [self.A.tile(NDC * 128) for _ in range(2)]
        wu_t = [self.A.tile(NDC * 128) for _ in range(2)]
        hrow = [self.A.tile(TC) for _ in range(2)]
        srow = [self.A.tile(TC) for _ in range(2)]
        hmax = self.S.tile(1)
        first_max = True
        for ic in range(NIC):
            wg = wg_t[ic % 2]
            wu = wu_t[ic % 2]
            self.dma_grouped("sync", wg.ap, self.dram["wgq"], 0, NDC,
                             ic * 128, 128,
                             reads=[self.dram_bufs["wgq"]], writes=[wg.buf])
            self.dma_grouped("sync", wu.ap, self.dram["wuq"], 0, NDC,
                             ic * 128, 128,
                             reads=[self.dram_bufs["wuq"]], writes=[wu.buf])
            for tc_i in range(NTCH):
                t0 = tc_i * TC
                psg = self.ps[tc_i % 2]
                psu = self.ps[2 + tc_i % 2]
                for hc in range(NDC):
                    lg = Tile(wg.ap[:, hc * 128:(hc + 1) * 128], wg.buf)
                    lu = Tile(wu.ap[:, hc * 128:(hc + 1) * 128], wu.buf)
                    rx = Tile(lnx.ap[:, hc * T + t0:hc * T + t0 + TC], lnx.buf)
                    self.mm(psg.ap[:, :], lg.ap, rx.ap, hc == 0, hc == NDC - 1,
                            reads=[lg.buf, rx.buf], writes=[psg.buf])
                    self.mm(psu.ap[:, :], lu.ap, rx.ap, hc == 0, hc == NDC - 1,
                            reads=[lu.buf, rx.buf], writes=[psu.buf])
                sg = srow[tc_i % 2]
                self.act(sg, psg, AF.Silu)
                hh = hrow[tc_i % 2]
                self.tt(hh, sg, psu, ALU.mult)
                if ic == 0 and tc_i == 0:
                    self.dbg(hh)
                pm = self.S.tile(1)
                self.red(pm, hh, op=ALU.max, absval=True)
                if first_max:
                    self.copy(hmax, pm, eng="vector")
                    first_max = False
                else:
                    self.tt(hmax, hmax, pm, ALU.max)
                self.dma("gpsimd", h_d.ap[ic * 128:(ic + 1) * 128, t0:t0 + TC],
                         hh.ap, reads=[hh.buf], writes=[h_d.buf])
        hmax_loc = self.pmax_to_scalar(hmax, "pm_l1f_h")
        ghmax = self.allreduce(ALU.max, hmax_loc)
        inv2, sc2 = self.quant_scales(ghmax, "L1clip_down")
        if stage < 4:
            self._copy_to_out(lnx, x_out)
            self.A.reset(mk0)
            self.S.reset(mks)
            return

        self.A.reset(mk0)
        wd_sb = self.A.tile(NIC * D)
        self.dma_grouped("sync", wd_sb.ap[:, 0:11 * D], self.dram["wdq"],
                         0, 11, 0, D,
                         reads=[self.dram_bufs["wdq"]], writes=[wd_sb.buf])
        self.dma_grouped("sync", wd_sb.ap[:, 11 * D:22 * D], self.dram["wdq"],
                         11, 11, 0, D,
                         reads=[self.dram_bufs["wdq"]], writes=[wd_sb.buf])
        hch = self.A.tile(NIC * TC)
        xch2 = self.A.tile(NDC * TC)
        res = [self.A.tile(TC)]
        res.append(res[0])
        qtmp2 = self.A.tile(TC)
        for tc_i in range(NTCH):
            t0 = tc_i * TC
            self.dma_grouped("sync", hch.ap, h_d.ap, 0, NIC, t0, TC,
                             reads=[h_d.buf], writes=[hch.buf])
            for ic in range(NIC):
                tt = Tile(hch.ap[:, ic * TC:(ic + 1) * TC], hch.buf)
                self.fake_quant_tile(tt, qtmp2, inv2, sc2)
            self.dma_grouped("sync", xch2.ap, x_in.ap, 0, NDC, t0, TC,
                             reads=[x_in.buf], writes=[xch2.buf])
            for dc in range(NDC):
                pso = self.ps[dc % 2]
                for ic in range(NIC):
                    lw = Tile(wd_sb.ap[:, ic * D + dc * 128:ic * D + (dc + 1) * 128],
                              wd_sb.buf)
                    rh = Tile(hch.ap[:, ic * TC:(ic + 1) * TC], hch.buf)
                    self.mm(pso.ap[:, :], lw.ap, rh.ap, ic == 0, ic == NIC - 1,
                            reads=[lw.buf, rh.buf], writes=[pso.buf])
                ro = res[dc % 2]
                xt = Tile(xch2.ap[:, dc * TC:(dc + 1) * TC], xch2.buf)
                self.tt(ro, xt, pso, ALU.add)
                self.dma("gpsimd", x_out.ap[dc * 128:(dc + 1) * 128, t0:t0 + TC],
                         ro.ap, reads=[ro.buf], writes=[x_out.buf])
        self.A.reset(mk0)
        self.S.reset(mks)



def _prep_device_inputs(inputs):
    f32 = np.float32
    cm = {}
    g = {}

    def add(name, arr):
        a = np.ascontiguousarray(arr, dtype=f32)
        assert a.ndim == 2
        g[name] = a

    add("c_ones_1x128", np.ones((1, 128), f32))
    add("c_ones_128x1", np.ones((128, 1), f32))
    add("c_actvals", np.tile(np.asarray(ACT_CONST_VALS, f32)[None, :], (128, 1)))
    add("c_ident", np.eye(128, dtype=f32))
    for j in range(4):
        p = np.arange(128)[:, None]
        i = np.arange(512)[None, :]
        add("c_mask%d" % j, (i >= j * 128 + p).astype(f32))
    def coltile(v, ncol):
        return np.ascontiguousarray(v.reshape(ncol, 128).T)

    for nm in ["wq1", "wk1", "wv1", "wo1", "wq2", "wk2", "wv2", "wo2",
               "wg1", "wu1", "wd1", "wg2", "wu2", "wd2"]:
        add("Lf_" + nm, inputs["L0_ffn_" + nm])
    add("Lf_priorw", inputs["L0_ffn_prior_w"])
    for nm in ["ln1_1", "ln2_1", "ln1_2", "ln2_2"]:
        add("Lf_" + nm, coltile(inputs["L0_ffn_" + nm], 8))
    add("Lf_priorn", coltile(inputs["L0_ffn_prior_norm_w"], 8))
    add("c_rep2x64", (np.arange(128)[None, :] % 64 ==
                      np.arange(64)[:, None]).astype(f32))
    add("c_iota_rep", np.tile(np.arange(1, 1025, dtype=f32)[None, :], (128, 1)))
    add("c_iota_col", (np.arange(8)[None, :] * 128 +
                       np.arange(128)[:, None] + 1).astype(f32))
    pos64 = np.arange(T, dtype=np.float64)
    inv64 = 1.0 / 10000.0 ** (np.arange(32, dtype=np.float64) / 32.0)
    inv32 = (1.0 / 10000.0 ** (np.arange(32, dtype=f32) / f32(32.0))).astype(f32)
    ang32 = (pos64.astype(f32)[None, :].T * inv32[None, :]).astype(f32)
    ang64 = pos64[:, None] * inv64[None, :]
    c32 = np.cos(ang32.astype(np.float64))
    s32 = np.sin(ang32.astype(np.float64))
    c64 = np.cos(ang64)
    s64 = np.sin(ang64)

    def hilo(v):
        hi = v.astype(f32)
        lo = (v - hi.astype(np.float64)).astype(f32)
        return hi, lo
    chi, clo = hilo(c64)
    shi, slo = hilo(s64)
    tab32 = np.concatenate([c32.T.astype(f32), s32.T.astype(f32)], axis=0)
    tabhi = np.concatenate([chi.T, shi.T], axis=0)
    tablo = np.concatenate([clo.T, slo.T], axis=0)
    add("c_tab32", tab32)
    add("c_tabhi", tabhi)
    add("c_tablo", tablo)
    add("c_tab32_tm", np.ascontiguousarray(tab32.T))
    add("c_tabhi_tm", np.ascontiguousarray(tabhi.T))
    add("c_tablo_tm", np.ascontiguousarray(tablo.T))
    add("L0aa_winT", inputs["L0_attn_a_W_in"].T)
    add("L0aa_wxT", inputs["L0_attn_a_W_xproj"].T)
    add("L0aa_woutT", inputs["L0_attn_a_W_out"].T)
    add("L0aa_wdt", coltile(inputs["L0_attn_a_W_dt"][:, 0], 16))
    add("L0aa_bdt", coltile(inputs["L0_attn_a_b_dt"], 16))
    add("L0aa_D", coltile(inputs["L0_attn_a_D"], 16))
    add("L0aa_convw", np.ascontiguousarray(
        inputs["L0_attn_a_conv_w"][:, 0, :].reshape(16, 128, 4)
        .transpose(1, 0, 2).reshape(128, 64)))
    add("L0aa_convb", coltile(inputs["L0_attn_a_conv_b"], 16))
    add("L0aa_alog", np.ascontiguousarray(
        inputs["L0_attn_a_A_log"].reshape(16, 128, 16)
        .transpose(1, 0, 2).reshape(128, 256)))
    add("c_rep16", (np.arange(1024)[None, :] // 64 ==
                    np.arange(16)[:, None]).astype(f32))
    add("L1aa_wq", inputs["L1_attn_a_W_Q"][:, :, 0].T)
    add("L1aa_wk", inputs["L1_attn_a_W_K"][:, :, 0].T)
    add("L1aa_wv", np.transpose(inputs["L1_attn_a_W_V"], (1, 0, 2)).reshape(1024, 1024))
    add("L1aa_wo", np.transpose(inputs["L1_attn_a_W_O"], (2, 1, 0)).reshape(1024, 1024))
    add("L1aa_w1", inputs["L1_attn_a_W1"])
    add("L1aa_w2", inputs["L1_attn_a_W2"])
    add("L1aa_b1", coltile(inputs["L1_attn_a_b1"], 32))
    add("L1aa_b2", coltile(inputs["L1_attn_a_b2"], 8))
    add("L1aa_ln1w", coltile(inputs["L1_attn_a_ln1_w"], 8))
    add("L1aa_ln1b", coltile(inputs["L1_attn_a_ln1_b"], 8))
    add("L1aa_ln2w", coltile(inputs["L1_attn_a_ln2_w"], 8))
    add("L1aa_ln2b", coltile(inputs["L1_attn_a_ln2_b"], 8))
    add("L0ab_wq", inputs["L0_attn_b_Wq"].T)
    add("L0ab_wk", inputs["L0_attn_b_Wk"].T)
    add("L0ab_wv", inputs["L0_attn_b_Wv"].T)
    add("L0ab_gamma", inputs["L0_attn_b_gamma"].reshape(1, 1))
    wqkv = inputs["L1_attn_b_w_qkv"]
    add("L1ab_wq", np.transpose(wqkv[0], (2, 0, 1)).reshape(1024, 1024))
    add("L1ab_wk", np.transpose(wqkv[1], (2, 0, 1)).reshape(1024, 1024))
    add("L1ab_wv", np.transpose(wqkv[2], (2, 0, 1)).reshape(1024, 1024))
    add("L1ab_wo", inputs["L1_attn_b_w_o"].T)
    add("L1w_gate", inputs["L1_ffn_w_gate"])
    add("L1w_gateT", inputs["L1_ffn_w_gate"].T)
    add("L1w_up", inputs["L1_ffn_w_up"])
    add("L1w_upT", inputs["L1_ffn_w_up"].T)
    add("L1w_down", inputs["L1_ffn_w_down"])
    add("L1w_downT", inputs["L1_ffn_w_down"].T)
    add("L1clip_x", inputs["L1_ffn_clip_x"].reshape(1, 1))
    add("L1clip_down", inputs["L1_ffn_clip_down"].reshape(1, 1))

    percore = []
    x = np.asarray(inputs["L0_attn_a_x"], dtype=f32)
    for c in range(NCORES):
        m = dict(g)
        m["x0"] = np.ascontiguousarray(x[c].T)
        percore.append(m)
    return percore


SUBLAYERS_ALL = ["l0attna", "l0ffn", "l0attnb", "l0ffn",
                 "l1attna", "l1ffn", "l1attnb", "l1ffn"]


def build(dev_meta, subset, ndbg=0):
    stack = contextlib.ExitStack()
    with stack:
        nc = bass.Bass("TRN2", target_bir_lowering=False)
        meta = {}
        for name, arr in dev_meta.items():
            meta[name] = (arr.shape, F32, "ExternalInput")
        meta["out_x"] = ((D, T), F32, "ExternalOutput")
        for i in range(ndbg):
            meta["dbg%d" % i] = ((128, 8192), F32, "ExternalOutput")
        mk = MK(nc, stack, meta)
        if mk.actconst_buf is not None and mk.actconst_buf.w is not None:
            for e in ENGS:
                mk.p.wait_tok(e, mk.actconst_buf.w)
        import os as _os
        if not _os.environ.get("NOROUND"):
            mk.setup_round()
        x_cur = Tile(mk.ext["x0"], Buf())
        ping = mk.scratch("x_ping", [D, T])
        pong = mk.scratch("x_pong", [D, T])
        bufs = [ping, pong]
        n = 0
        for name in subset:
            dst = bufs[n % 2]
            if name == "ccprobe":
                ones = mk.const("c_ones_128x1")
                g = mk.allreduce(ALU.max, Tile(ones.ap[0:1, 0:1], ones.buf))
                mk.dma("gpsimd", dst.ap[0:1, 0:1], g.ap[0:1, 0:1],
                       reads=[g.buf], writes=[dst.buf])
            elif name == "l1ffn":
                import os
                mk.emit_l1ffn(x_cur, dst, first=("l1ffn" not in subset[:n]),
                              stage=float(os.environ.get("STAGE", "9")))
            elif name == "l1attnb":
                mk.emit_l1attnb(x_cur, dst)
            elif name == "l0attnb":
                mk.emit_l0attnb(x_cur, dst)
            elif name == "l1attna":
                mk.emit_l1attna(x_cur, dst)
            elif name == "l0attna":
                mk.emit_l0attna(x_cur, dst)
            elif name == "l0ffn":
                mk.emit_l0ffn(x_cur, dst)
            else:
                raise NotImplementedError(name)
            x_cur = dst
            n += 1
        out = Tile(mk.ext["out_x"], Buf())
        ocp = mk.A.tile(NDC * TC)
        toks = []
        for tc_i in range(NTCH):
            t0 = tc_i * TC
            mk.dma_grouped("sync", ocp.ap, x_cur.ap, 0, NDC, t0, TC,
                           reads=[x_cur.buf], writes=[ocp.buf])
            toks.append(mk.p.op("gpsimd", lambda e, t0=t0, src=ocp.ap: e.dma_start(
                out=out.ap.rearrange("(c p) t -> p c t", p=128)[:, :, t0:t0 + TC],
                in_=src.rearrange("p (c t) -> p c t", t=TC)),
                reads=[ocp.buf], writes=[out.buf], dma=True))
        mk.p.wait_tok("gpsimd", toks[-1])
        mk.p.playback()
    return nc


def kernel(**inputs):
    percore = _prep_device_inputs(inputs)
    nc = build(percore[0], SUBLAYERS_ALL)
    res = run_bass_kernel_spmd(nc, percore, core_ids=list(range(NCORES)))
    out = np.stack([np.ascontiguousarray(r["out_x"].T) for r in res.results])
    return out.astype(np.float32)
